# version 01
# baseline (speedup 1.0000x reference)
import numpy as np
import concourse.bass as bass
import concourse.mybir as mybir
from concourse.bass_utils import run_bass_kernel_spmd

F32 = mybir.dt.float32
BF16 = mybir.dt.bfloat16
I32 = mybir.dt.int32
AF = mybir.ActivationFunctionType
ALU = mybir.AluOpType
AX = mybir.AxisListType

NCORES = 8



class Sem:
    def __init__(self, handle):
        self.h = handle
        self.n = 0

    def bump(self, k=1):
        self.n += k
        return self.n


class Prog:
    ENGINES = ("sync", "gpsimd", "tensor", "vector", "scalar")

    def __init__(self):
        self.streams = {e: [] for e in self.ENGINES}

    def on(self, eng, fn):
        self.streams[eng].append(fn)

    def wait(self, eng, sem, val):
        self.streams[eng].append(lambda e, s=sem.h, v=val: e.wait_ge(s, v))

    def play(self, block):
        for name in self.ENGINES:
            fns = self.streams[name]
            getattr(block, name)(lambda e, fns=fns: [f(e) for f in fns])


def ceil_div(a, b):
    return (a + b - 1) // b



def emit_ln(p, s_x, s_rows, s_tmp, s_oc, s_or, ps_stats, ps_b, s_sq,
            sem_pe, sem_dv, sem_ac, KC, T, D, out_emit, eps_ap=None):
    TT = T // 512

    for c in range(KC):
        pe_gate = sem_pe.n if c >= 1 else None

        def fsq(e, c=c, pe_gate=pe_gate):
            if pe_gate is not None:
                e.wait_ge(sem_pe.h, pe_gate)
            e.activation(s_sq[0], s_x[:, c * T:(c + 1) * T], AF.Square
                         ).then_inc(sem_ac.h, 1)
        p.on("scalar", fsq)
        ac_ms = sem_ac.bump()

        def fmm(e, c=c, ac_ms=ac_ms):
            e.wait_ge(sem_ac.h, ac_ms)
            ins = None
            for tt in range(TT):
                e.matmul(ps_stats[tt][0:1, :], s_oc[:],
                         s_x[:, c * T + tt * 512: c * T + (tt + 1) * 512],
                         start=(c == 0), stop=(c == KC - 1),
                         tile_position=(0, 0))
                ins = e.matmul(ps_stats[tt][32:33, :], s_oc[:],
                               s_sq[0][:, tt * 512:(tt + 1) * 512],
                               start=(c == 0), stop=(c == KC - 1),
                               tile_position=(0, 32))
            ins.then_inc(sem_pe.h, 1)
        p.on("tensor", fmm)
        sem_pe.bump()

    stats_done = sem_pe.n

    def frows(e, need=stats_done):
        e.wait_ge(sem_pe.h, need)
        for tt in range(TT):
            e.tensor_scalar_mul(s_rows[0:1, tt * 512:(tt + 1) * 512],
                                ps_stats[tt][0:1, :], 1.0 / D)
        ins = None
        for tt in range(TT):
            ins = e.tensor_scalar_mul(
                s_rows[0:1, T + tt * 512: T + (tt + 1) * 512],
                ps_stats[tt][32:33, :], 1.0 / D)
        ins.then_inc(sem_dv.h, 1)
    p.on("vector", frows)
    rows_done = sem_dv.bump()

    def fm2(e, need=rows_done):
        e.wait_ge(sem_dv.h, need)
        e.activation(s_rows[0:1, 2 * T:3 * T], s_rows[0:1, 0:T], AF.Square
                     ).then_inc(sem_ac.h, 1)
    p.on("scalar", fm2)
    m2_done = sem_ac.bump()

    def fvar(e, need=m2_done):
        e.wait_ge(sem_ac.h, need)
        e.tensor_sub(s_rows[0:1, 3 * T:4 * T], s_rows[0:1, T:2 * T],
                     s_rows[0:1, 2 * T:3 * T]).then_inc(sem_dv.h, 1)
    p.on("vector", fvar)
    var_done = sem_dv.bump()

    def frstd(e, need=var_done):
        e.wait_ge(sem_dv.h, need)
        e.activation(s_rows[0:1, 2 * T:3 * T], s_rows[0:1, 3 * T:4 * T], AF.Ln,
                     bias=eps_ap).then_inc(sem_ac.h, 1)
    p.on("scalar", frstd)
    ln_done = sem_ac.bump()

    def fexp(e, need=ln_done):
        e.wait_ge(sem_ac.h, need)
        e.activation(s_rows[0:1, 3 * T:4 * T], s_rows[0:1, 2 * T:3 * T], AF.Exp,
                     scale=-0.5).then_inc(sem_ac.h, 1)
    p.on("scalar", fexp)
    rstd_done = sem_ac.bump()

    def fnegmr(e, need=rstd_done):
        e.wait_ge(sem_ac.h, need)
        e.scalar_tensor_tensor(s_rows[0:1, 4 * T:5 * T], s_rows[0:1, 0:T],
                               -1.0, s_rows[0:1, 3 * T:4 * T],
                               ALU.mult, ALU.mult).then_inc(sem_dv.h, 1)
    p.on("vector", fnegmr)
    negmr_done = sem_dv.bump()

    def fbcast(e, need=negmr_done):
        e.wait_ge(sem_dv.h, need)
        ins = None
        for tt in range(TT):
            e.matmul(ps_b[tt][0][:, :], s_or[:],
                     s_rows[0:1, 3 * T + tt * 512: 3 * T + (tt + 1) * 512],
                     start=True, stop=True)
            ins = e.matmul(ps_b[tt][1][:, :], s_or[:],
                           s_rows[0:1, 4 * T + tt * 512: 4 * T + (tt + 1) * 512],
                           start=True, stop=True)
        ins.then_inc(sem_pe.h, 1)
    p.on("tensor", fbcast)
    bcast_done = sem_pe.bump()

    fadd_ms = []
    for c in range(KC):
        for tt in range(TT):
            def fmul(e, c=c, tt=tt, need=bcast_done, war=sem_dv.n):
                if c == 0 and tt == 0:
                    e.wait_ge(sem_pe.h, need)
                else:
                    e.wait_ge(sem_dv.h, war)
                e.tensor_mul(s_tmp[:, :512],
                             s_x[:, c * T + tt * 512: c * T + (tt + 1) * 512],
                             ps_b[tt][0][:, :]).then_inc(sem_dv.h, 1)
            p.on("vector", fmul)
            mul_ms = sem_dv.bump()

            def fadd(e, c=c, tt=tt, need=mul_ms):
                e.wait_ge(sem_dv.h, need)
                out_emit(e, c, tt, s_tmp[:, :512], ps_b[tt][1][:, :]
                         ).then_inc(sem_dv.h, 1)
            p.on("vector", fadd)
            fadd_ms.append(sem_dv.bump())
    return sem_dv.n, fadd_ms



def build_launch_a(D=2048, T=1024, HD=32):
    KC = D // 128
    TT = T // 512
    NH = D // HD
    NTC = T // 128
    NQ = NH // 4
    NB = 2 if NQ > 1 else 1
    QPB = ceil_div(NQ, NB)
    KVW = min(512, D)
    OFH = D // KVW

    nc = bass.Bass("TRN2", target_bir_lowering=False, debug=False)
    x_cm = nc.dram_tensor("x_cm", [D, T], F32, kind="ExternalInput").ap()
    wq_t = nc.dram_tensor("wq_t", [KC, KC, 128, 128], F32, kind="ExternalInput").ap()
    wkv_k = nc.dram_tensor("wkv_k", [D, D], F32, kind="ExternalInput").ap()
    wkv_v = nc.dram_tensor("wkv_v", [D, D], F32, kind="ExternalInput").ap()
    fw_t = nc.dram_tensor("fw_t", [KC, KC, 128, 128], F32, kind="ExternalInput").ap()
    fscale_pm = nc.dram_tensor("fscale_pm", [128, KC], F32, kind="ExternalInput").ap()
    fbias_pm = nc.dram_tensor("fbias_pm", [128, KC], F32, kind="ExternalInput").ap()
    ones_col = nc.dram_tensor("ones_col", [128, 1], F32, kind="ExternalInput").ap()
    ones_row = nc.dram_tensor("ones_row", [1, 128], F32, kind="ExternalInput").ap()
    cst = nc.dram_tensor("cst", [1, 4], F32, kind="ExternalInput").ap()

    q_out = nc.dram_tensor("q_out", [128, KC * T], BF16, kind="ExternalOutput").ap()
    vk_out = nc.dram_tensor("vk_out", [128, NB * QPB * 33], F32, kind="ExternalOutput").ap()
    g_out = nc.dram_tensor("g_out", [128, KC * T], F32, kind="ExternalOutput").ap()

    from contextlib import ExitStack
    with ExitStack() as ctx:
        e_ = ctx.enter_context
        W1 = max(2 * KC * T, KC * D)
        ar1 = e_(nc.sbuf_tensor("ar1", [128, W1], BF16))
        s_x = ar1.ap()[:, 0:2 * KC * T].bitcast(F32)
        s_rhs = ar1.ap()[:, 0:KC * D]
        ar2 = e_(nc.sbuf_tensor("ar2", [128, KC * T], BF16))
        s_q = ar2.ap()
        s_k = ar2.ap()
        ar3 = e_(nc.sbuf_tensor("ar3", [128, T + 2048], F32))
        s_sq = [ar3.ap()[:, 0:T]]
        s_gt = [ar3.ap()[:, T + i * 512: T + (i + 1) * 512] for i in range(2)]
        s_gu = [ar3.ap()[:, T + 1024 + i * 512: T + 1024 + (i + 1) * 512] for i in range(2)]

        s_xn = e_(nc.sbuf_tensor("s_xn", [128, KC * T], BF16))
        s_vp = e_(nc.sbuf_tensor("s_vp", [128, NTC * NH * 33], BF16))
        s_wslab = [e_(nc.sbuf_tensor(f"s_wslab{i}", [128, KC * 128], BF16)) for i in range(2)]
        s_rows = e_(nc.sbuf_tensor("s_rows", [1, 5 * T], F32))
        s_tmp = e_(nc.sbuf_tensor("s_tmp", [128, 512], F32))
        s_oc = e_(nc.sbuf_tensor("s_oc", [128, 1], F32))
        s_or = e_(nc.sbuf_tensor("s_or", [1, 128], F32))
        s_vk = e_(nc.sbuf_tensor("s_vk", [128, NB * QPB * 33], F32))
        s_fs = e_(nc.sbuf_tensor("s_fs", [128, KC], F32))
        s_fb = e_(nc.sbuf_tensor("s_fb", [128, KC], F32))
        s_cst = e_(nc.sbuf_tensor("s_cst", [1, 4], F32))

        psum = [e_(nc.psum_tensor(f"ps{i}", [128, 512], F32)) for i in range(8)]

        sem_x = Sem(e_(nc.semaphore("sem_x")))
        sem_wa = Sem(e_(nc.semaphore("sem_wa")))
        sem_wb = Sem(e_(nc.semaphore("sem_wb")))
        sem_wr = Sem(e_(nc.semaphore("sem_wr")))
        sem_pe = Sem(e_(nc.semaphore("sem_pe")))
        sem_dv = Sem(e_(nc.semaphore("sem_dv")))
        sem_ac = Sem(e_(nc.semaphore("sem_ac")))
        sem_stq = Sem(e_(nc.semaphore("sem_stq")))
        sem_stv = Sem(e_(nc.semaphore("sem_stv")))
        sem_sg = [Sem(e_(nc.semaphore(f"sem_sg{i}"))) for i in range(2)]
        block = e_(nc.Block())

        p = Prog()
        sem_w = [sem_wa, sem_wb]

        def fload(e):
            e.dma_start(out=s_x.rearrange("p (c t) -> p c t", t=T),
                        in_=x_cm.rearrange("(c p) t -> p c t", p=128)
                        ).then_inc(sem_x.h, 16)
            e.dma_start(out=s_oc[:, :], in_=ones_col).then_inc(sem_x.h, 16)
            e.dma_start(out=s_or[:, :], in_=ones_row).then_inc(sem_x.h, 16)
            e.dma_start(out=s_fs[:, :], in_=fscale_pm).then_inc(sem_x.h, 16)
            e.dma_start(out=s_fb[:, :], in_=fbias_pm).then_inc(sem_x.h, 16)
            e.dma_start(out=s_cst[:, :], in_=cst).then_inc(sem_x.h, 16)
        p.on("sync", fload)
        x_ready = sem_x.bump(16 * 6)

        p.wait("scalar", sem_x, x_ready)
        p.wait("tensor", sem_x, x_ready)
        p.wait("vector", sem_x, x_ready)

        def xn_emit(e, c, tt, prod_ap, negmr_ap):
            return e.tensor_add(s_xn[:, c * T + tt * 512: c * T + (tt + 1) * 512],
                                prod_ap, negmr_ap)
        xn_done, _ = emit_ln(p, s_x, s_rows, s_tmp, s_oc, s_or,
                             [psum[6], psum[7]],
                             [[psum[2], psum[3]], [psum[4], psum[5]]],
                             s_sq, sem_pe, sem_dv, sem_ac, KC, T, D, xn_emit,
                             eps_ap=s_cst[0:1, 0:1])

        def slab_loader(src_tiled, oc, pe_gate):
            sem = sem_w[oc % 2]
            def f(e, src_tiled=src_tiled, oc=oc, pe_gate=pe_gate, sem=sem):
                if pe_gate is not None:
                    e.wait_ge(sem_pe.h, pe_gate)
                e.dma_start(out=s_wslab[oc % 2][:].rearrange("p (k m) -> p k m", m=128),
                            in_=src_tiled[oc].rearrange("k p m -> p k m")
                            ).then_inc(sem.h, 16)
            return f, sem.bump(16)

        q_evac_ms = []
        q_pe_oc_end = []
        for oc in range(KC):
            gate = q_pe_oc_end[oc - 2] if oc >= 2 else None
            f, ready = slab_loader(wq_t, oc, gate)
            p.on("gpsimd", f)
            wsem = sem_w[oc % 2]

            for tt in range(TT):
                g = oc * TT + tt
                bank = psum[g % 2]
                bank_gate = q_evac_ms[g - 2] if g >= 2 else None

                def fmm(e, oc=oc, tt=tt, bank=bank, ready=ready, wsem=wsem,
                        bank_gate=bank_gate, xn_need=xn_done):
                    e.wait_ge(wsem.h, ready)
                    if oc == 0 and tt == 0:
                        e.wait_ge(sem_dv.h, xn_need)
                    if bank_gate is not None:
                        e.wait_ge(sem_ac.h, bank_gate)
                    ins = None
                    for k in range(KC):
                        ins = e.matmul(bank[:, :],
                                       s_wslab[oc % 2][:, k * 128:(k + 1) * 128],
                                       s_xn[:, k * T + tt * 512: k * T + (tt + 1) * 512],
                                       start=(k == 0), stop=(k == KC - 1))
                    ins.then_inc(sem_pe.h, 1)
                p.on("tensor", fmm)
                ms = sem_pe.bump()

                def fevac(e, oc=oc, tt=tt, bank=bank, ms=ms):
                    e.wait_ge(sem_pe.h, ms)
                    e.activation(s_q[:, oc * T + tt * 512: oc * T + (tt + 1) * 512],
                                 bank[:, :], AF.Relu).then_inc(sem_ac.h, 1)
                p.on("scalar", fevac)
                q_evac_ms.append(sem_ac.bump())
            q_pe_oc_end.append(sem_pe.n)
        q_done_ac = sem_ac.n

        def fstq(e, need=q_done_ac):
            e.wait_ge(sem_ac.h, need)
            e.dma_start(out=q_out, in_=s_q[:, :]).then_inc(sem_stq.h, 16)
        p.on("sync", fstq)
        q_stored = sem_stq.bump(16)

        def fones(e):
            e.memset(s_vp[:, :], 1.0).then_inc(sem_dv.h, 1)
        p.on("vector", fones)
        vp_ones_done = sem_dv.bump()

        kv_prev_ac = None
        kv_prev_dv = None
        for half in range(2):
            src = wkv_k if half == 0 else wkv_v
            rhs_pe_gate = sem_pe.n if half == 1 else None

            def frhs(e, src=src, rhs_pe_gate=rhs_pe_gate, xn_need=xn_done):
                if rhs_pe_gate is not None:
                    e.wait_ge(sem_pe.h, rhs_pe_gate)
                else:
                    e.wait_ge(sem_dv.h, xn_need)
                e.dma_start(out=s_rhs.rearrange("p (c n) -> p c n", n=D),
                            in_=src.rearrange("(c p) n -> p c n", p=128)
                            ).then_inc(sem_wr.h, 16)
            p.on("gpsimd", frhs)
            rhs_ready = sem_wr.bump(16)

            for tc in range(NTC):
                gate_ac = kv_prev_ac
                gate_dv = kv_prev_dv if kv_prev_dv is not None else vp_ones_done

                def fkv(e, tc=tc, rhs_ready=rhs_ready, gate_ac=gate_ac,
                        gate_dv=gate_dv):
                    e.wait_ge(sem_wr.h, rhs_ready)
                    e.wait_ge(sem_dv.h, gate_dv)
                    if gate_ac is not None:
                        e.wait_ge(sem_ac.h, gate_ac)
                    ins = None
                    for k in range(KC):
                        lhsT = s_xn[:, k * T + tc * 128:k * T + (tc + 1) * 128]
                        for of in range(OFH):
                            ins = e.matmul(psum[4 + of][:, 0:KVW], lhsT,
                                           s_rhs[:, k * D + of * KVW:k * D + (of + 1) * KVW],
                                           start=(k == 0), stop=(k == KC - 1))
                    ins.then_inc(sem_pe.h, 1)
                p.on("tensor", fkv)
                ms = sem_pe.bump()

                if half == 0:
                    def fev(e, tc=tc, ms=ms, qst=q_stored):
                        e.wait_ge(sem_pe.h, ms)
                        e.wait_ge(sem_stq.h, qst)
                        ins = None
                        for of in range(OFH):
                            ins = e.activation(
                                s_k[:, tc * D + of * KVW: tc * D + (of + 1) * KVW],
                                psum[4 + of][:, 0:KVW], AF.Relu)
                        ins.then_inc(sem_ac.h, 1)
                    p.on("scalar", fev)
                    kv_prev_ac = sem_ac.bump()
                else:
                    HPB = KVW // HD
                    def fev(e, tc=tc, ms=ms):
                        e.wait_ge(sem_pe.h, ms)
                        ins = None
                        for of in range(OFH):
                            pview = psum[4 + of][:, 0:KVW].rearrange(
                                "p (h w) -> p h w", w=HD)
                            base = tc * NH * 33 + of * HPB * 33
                            oview = s_vp[:, base: base + HPB * 33].rearrange(
                                "p (h w) -> p h w", w=33)[:, :, 0:HD]
                            ins = e.tensor_copy(oview, pview)
                        ins.then_inc(sem_dv.h, 1)
                    p.on("vector", fev)
                    kv_prev_dv = sem_dv.bump()
        kv_done_ac = sem_ac.n
        kv_done_dv = sem_dv.n

        def fvk(e, ac_need=kv_done_ac, dv_need=kv_done_dv):
            e.wait_ge(sem_ac.h, ac_need)
            e.wait_ge(sem_dv.h, dv_need)
            ins = None
            for h in range(NH):
                l, q = h % 4, h // 4
                b, qc = q // QPB, (q % QPB) * 33
                for tc in range(NTC):
                    lhsT = s_k[:, tc * D + h * HD: tc * D + (h + 1) * HD]
                    rhs = s_vp[:, tc * NH * 33 + h * 33: tc * NH * 33 + (h + 1) * 33]
                    ins = e.matmul(psum[b][32 * l:32 * l + 32, qc:qc + 33], lhsT, rhs,
                                   start=(tc == 0), stop=(tc == NTC - 1),
                                   tile_position=(0, 32 * l))
            ins.then_inc(sem_pe.h, 1)
        p.on("tensor", fvk)
        vk_ms = sem_pe.bump()

        def fvk_ev(e, ms=vk_ms):
            e.wait_ge(sem_pe.h, ms)
            ins = None
            for b in range(NB):
                ins = e.tensor_copy(s_vk[:, b * QPB * 33:(b + 1) * QPB * 33],
                                    psum[b][:, 0:QPB * 33])
            ins.then_inc(sem_dv.h, 1)
        p.on("vector", fvk_ev)
        vk_ev = sem_dv.bump()

        def fvk_st(e, need=vk_ev):
            e.wait_ge(sem_dv.h, need)
            e.dma_start(out=vk_out, in_=s_vk[:, :]).then_inc(sem_stv.h, 16)
        p.on("sync", fvk_st)
        sem_stv.bump(16)

        ffn_dv_ms = []
        ffn_st_ms = []
        ffn_pe_oc_end = []
        for oc in range(KC):
            gate = ffn_pe_oc_end[oc - 2] if oc >= 2 else q_pe_oc_end[-1]
            f, ready = slab_loader(fw_t, oc, gate)
            p.on("gpsimd", f)
            wsem = sem_w[oc % 2]

            for tt in range(TT):
                g = oc * TT + tt
                bank = psum[2 + g % 2]
                bank_gate = ffn_dv_ms[g - 2] if g >= 2 else xn_done

                def fmm(e, oc=oc, tt=tt, bank=bank, ready=ready, wsem=wsem,
                        bank_gate=bank_gate, vk_need=vk_ms):
                    e.wait_ge(wsem.h, ready)
                    e.wait_ge(sem_dv.h, bank_gate)
                    if oc == 0 and tt == 0:
                        e.wait_ge(sem_pe.h, vk_need)
                    ins = None
                    for k in range(KC):
                        ins = e.matmul(bank[:, :],
                                       s_wslab[oc % 2][:, k * 128:(k + 1) * 128],
                                       s_xn[:, k * T + tt * 512: k * T + (tt + 1) * 512],
                                       start=(k == 0), stop=(k == KC - 1))
                    ins.then_inc(sem_pe.h, 1)
                p.on("tensor", fmm)
                ms = sem_pe.bump()

                gbuf = s_gt[g % 2]
                sgs = sem_sg[g % 2]
                gbuf_gate = sgs.n if g >= 2 else None

                def fsc(e, oc=oc, bank=bank, ms=ms, gbuf=gbuf, sgs=sgs,
                        gbuf_gate=gbuf_gate):
                    e.wait_ge(sem_pe.h, ms)
                    if gbuf_gate is not None:
                        e.wait_ge(sgs.h, gbuf_gate)
                    e.tensor_scalar(gbuf, bank[:, :], s_fs[:, oc:oc + 1],
                                    s_fb[:, oc:oc + 1], ALU.mult, ALU.add
                                    ).then_inc(sem_dv.h, 1)
                p.on("vector", fsc)
                dv_ms = sem_dv.bump()
                ffn_dv_ms.append(dv_ms)

                gub = s_gu[g % 2]

                def fsilu(e, gbuf=gbuf, gub=gub, need=dv_ms, war=gbuf_gate):
                    e.wait_ge(sem_dv.h, need)
                    if war is not None:
                        e.wait_ge(sgs.h, war)
                    e.activation(gub, gbuf, AF.Silu).then_inc(sem_ac.h, 1)
                p.on("scalar", fsilu)
                ac_ms = sem_ac.bump()

                def fst(e, oc=oc, tt=tt, gub=gub, need=ac_ms, sgs=sgs):
                    e.wait_ge(sem_ac.h, need)
                    e.dma_start(out=g_out[:, oc * T + tt * 512: oc * T + (tt + 1) * 512],
                                in_=gub).then_inc(sgs.h, 16)
                p.on("sync", fst)
                ffn_st_ms.append(sgs.bump(16))
            ffn_pe_oc_end.append(sem_pe.n)

        p.wait("sync", sem_stq, sem_stq.n)
        p.wait("sync", sem_stv, sem_stv.n)
        for i in range(2):
            p.wait("sync", sem_sg[i], sem_sg[i].n)
        p.play(block)
    return nc



def build_launch_b(D=2048, T=1024, HD=32, stage=4):
    KC = D // 128
    TT = T // 512
    NH = D // HD
    NTC = T // 128
    NQ = NH // 4
    NB = 2 if NQ > 1 else 1
    QPB = ceil_div(NQ, NB)
    VKW = NB * QPB * 33
    HPL = NH // 4
    HPG = min(8, HPL)
    NHLF = HPL // HPG

    nc = bass.Bass("TRN2", target_bir_lowering=False, debug=False)
    x_cm = nc.dram_tensor("x_cm", [D, T], F32, kind="ExternalInput").ap()
    q_in = nc.dram_tensor("q_in", [128, KC * T], BF16, kind="ExternalInput").ap()
    vk_s = nc.dram_tensor("vk_s", [128, VKW], F32, kind="ExternalInput").ap()
    vk_p = nc.dram_tensor("vk_p", [128, VKW], F32, kind="ExternalInput").ap()
    g_in = nc.dram_tensor("g_in", [128, KC * T], F32, kind="ExternalInput").ap()
    wp_t = nc.dram_tensor("wp_t", [KC, KC, 128, 128], F32, kind="ExternalInput").ap()
    bproj_pm = nc.dram_tensor("bproj_pm", [128, KC], F32, kind="ExternalInput").ap()
    gam_pm = nc.dram_tensor("gam_pm", [128, KC], F32, kind="ExternalInput").ap()
    bet_pm = nc.dram_tensor("bet_pm", [128, KC], F32, kind="ExternalInput").ap()
    ident = nc.dram_tensor("ident", [128, 128], BF16, kind="ExternalInput").ap()
    ones_col = nc.dram_tensor("ones_col", [128, 1], F32, kind="ExternalInput").ap()
    ones_row = nc.dram_tensor("ones_row", [1, 128], F32, kind="ExternalInput").ap()
    cst = nc.dram_tensor("cst", [1, 4], F32, kind="ExternalInput").ap()
    cst_col = nc.dram_tensor("cst_col", [128, 2], F32, kind="ExternalInput").ap()

    xnew_out = nc.dram_tensor("xnew_out", [128, KC * T], F32, kind="ExternalOutput").ap()
    xn2_out = nc.dram_tensor("xn2_out", [128, KC * T], F32, kind="ExternalOutput").ap()
    xln_out = nc.dram_tensor("xln_out", [128, KC * T], F32, kind="ExternalOutput").ap()

    from contextlib import ExitStack
    with ExitStack() as ctx:
        e_ = ctx.enter_context
        s_xnew = e_(nc.sbuf_tensor("s_xnew", [128, KC * T], F32))
        ar2 = e_(nc.sbuf_tensor("ar2", [128, 2 * KC * T + 128], BF16))
        s_q = ar2.ap()[:, 0:KC * T]
        s_att = ar2.ap()[:, KC * T:2 * KC * T + 128]
        s_acm = ar2.ap()[:, 0:KC * T]
        s_xn2 = ar2.ap()[:, 0:2 * KC * T].bitcast(F32)
        s_wslab = [e_(nc.sbuf_tensor(f"s_wslab{i}", [128, KC * 128], BF16)) for i in range(2)]
        s_sq = [e_(nc.sbuf_tensor("s_sq0", [128, T], F32)).ap()]
        s_rows = e_(nc.sbuf_tensor("s_rows", [1, 5 * T], F32))
        s_tmp = e_(nc.sbuf_tensor("s_tmp", [128, 512], F32))
        s_oc = e_(nc.sbuf_tensor("s_oc", [128, 1], F32))
        s_or = e_(nc.sbuf_tensor("s_or", [1, 128], F32))
        s_id = e_(nc.sbuf_tensor("s_id", [128, 128], BF16))
        s_vk = e_(nc.sbuf_tensor("s_vk", [128, 2 * VKW], F32))
        s_vkb = e_(nc.sbuf_tensor("s_vkb", [128, VKW], BF16))
        s_den = e_(nc.sbuf_tensor("s_den", [128, HPG], F32))
        s_rden = e_(nc.sbuf_tensor("s_rden", [128, HPG], F32))
        s_bp = e_(nc.sbuf_tensor("s_bp", [128, KC], F32))
        s_gam = e_(nc.sbuf_tensor("s_gam", [128, KC], F32))
        s_bet = e_(nc.sbuf_tensor("s_bet", [128, KC], F32))
        s_cst = e_(nc.sbuf_tensor("s_cst", [1, 4], F32))
        s_cc = e_(nc.sbuf_tensor("s_cc", [128, 2], F32))
        s_pt = [e_(nc.sbuf_tensor(f"s_pt{i}", [128, 512], F32)) for i in range(2)]
        s_mt = e_(nc.sbuf_tensor("s_mt", [128, 512], F32))
        s_xt = [e_(nc.sbuf_tensor(f"s_xt{i}", [128, 512], F32)) for i in range(2)]
        s_gt2 = [e_(nc.sbuf_tensor(f"s_gt2{i}", [128, 512], F32)) for i in range(2)]
        s_l1 = [e_(nc.sbuf_tensor(f"s_l1{i}", [128, 512], F32)) for i in range(2)]
        s_l2 = [e_(nc.sbuf_tensor(f"s_l2{i}", [128, 512], F32)) for i in range(2)]

        ps32 = [e_(nc.psum_tensor(f"ps{i}", [128, 512], F32)) for i in range(6)]
        psb = [e_(nc.psum_tensor(f"psb{i}", [128, 1024], BF16)) for i in range(2)]

        sem_x = Sem(e_(nc.semaphore("sem_x")))
        sem_wa = Sem(e_(nc.semaphore("sem_wa")))
        sem_wb = Sem(e_(nc.semaphore("sem_wb")))
        sem_xl = [Sem(e_(nc.semaphore(f"sem_xl{i}"))) for i in range(2)]
        sem_pe = Sem(e_(nc.semaphore("sem_pe")))
        sem_dv = Sem(e_(nc.semaphore("sem_dv")))
        sem_ac = Sem(e_(nc.semaphore("sem_ac")))
        sem_dx = Sem(e_(nc.semaphore("sem_dx")))
        sem_s1 = Sem(e_(nc.semaphore("sem_s1")))
        sem_s2 = Sem(e_(nc.semaphore("sem_s2")))
        sem_s3 = [Sem(e_(nc.semaphore(f"sem_s3{i}"))) for i in range(2)]
        block = e_(nc.Block())

        p = Prog()
        sem_w = [sem_wa, sem_wb]

        def fload(e):
            for dst, src in ((s_q, q_in), (s_vk[:, 0:VKW], vk_s),
                             (s_vk[:, VKW:2 * VKW], vk_p),
                             (s_id[:, :], ident), (s_oc[:, :], ones_col),
                             (s_or[:, :], ones_row), (s_cst[:, :], cst),
                             (s_bp[:, :], bproj_pm), (s_gam[:, :], gam_pm),
                             (s_bet[:, :], bet_pm), (s_cc[:, :], cst_col)):
                e.dma_start(out=dst, in_=src).then_inc(sem_x.h, 16)
        p.on("sync", fload)
        x_ready = sem_x.bump(176)

        for eng in ("tensor", "vector", "scalar"):
            p.wait(eng, sem_x, x_ready)

        def fvkadd(e):
            e.tensor_add(s_vkb[:, :], s_vk[:, 0:VKW], s_vk[:, VKW:2 * VKW]
                         ).then_inc(sem_dv.h, 1)
        p.on("vector", fvkadd)
        vkb_done = sem_dv.bump()

        def vkcol(h):
            l, qd = h % 4, h // 4
            return qd // QPB * (QPB * 33) + (qd % QPB) * 33

        div_ms = []
        exp_ms = []
        grp_idx = 0
        for tc in range(NTC):
            for l in range(4):
              for half in range(NHLF):
                bank = ps32[grp_idx % 6]
                bank_gate = div_ms[grp_idx - 6] if grp_idx >= 6 else None
                heads = [l + 4 * (half * HPG + i) for i in range(HPG)]

                def fmm(e, tc=tc, l=l, heads=tuple(heads), bank=bank,
                        bank_gate=bank_gate, vkb_need=vkb_done, first=(grp_idx == 0)):
                    if bank_gate is not None:
                        e.wait_ge(sem_dv.h, bank_gate)
                    elif first:
                        e.wait_ge(sem_dv.h, vkb_need)
                    ins = None
                    for hh, h in enumerate(heads):
                        lhsT = s_q[32 * l:32 * l + 32,
                                   (h // 4) * T + tc * 128:(h // 4) * T + (tc + 1) * 128]
                        rhs = s_vkb[32 * l:32 * l + 32, vkcol(h):vkcol(h) + 33]
                        ins = e.matmul(bank[:, hh * 33:hh * 33 + 33], lhsT, rhs,
                                       start=True, stop=True,
                                       tile_position=(32 * l, 0))
                    ins.then_inc(sem_pe.h, 1)
                p.on("tensor", fmm)
                pe_ms = sem_pe.bump()

                prev_exp = exp_ms[-1] if exp_ms else None
                prev_div = div_ms[-1] if div_ms else None

                def fln(e, bank=bank, pe_ms=pe_ms, prev_exp=prev_exp):
                    e.wait_ge(sem_pe.h, pe_ms)
                    if prev_exp is not None:
                        e.wait_ge(sem_ac.h, prev_exp)
                    e.activation(s_den[:, :], bank[:, 32:HPG * 33:33], AF.Ln,
                                 bias=s_cc[:, 0:1]).then_inc(sem_ac.h, 1)
                p.on("scalar", fln)
                lnm = sem_ac.bump()

                def fexp(e, need=lnm, war=prev_div):
                    e.wait_ge(sem_ac.h, need)
                    if war is not None:
                        e.wait_ge(sem_dv.h, war)
                    e.activation(s_rden[:, :], s_den[:, :], AF.Exp, scale=-1.0
                                 ).then_inc(sem_ac.h, 1)
                p.on("scalar", fexp)
                d2 = sem_ac.bump()
                exp_ms.append(d2)

                def fdiv3(e, tc=tc, l=l, half=half, bank=bank, need=d2):
                    e.wait_ge(sem_ac.h, need)
                    nums = bank[:, 0:HPG * 33].rearrange("p (h w) -> p h w", w=33)[:, :, 0:HD]
                    rden = s_rden[:, :].unsqueeze(2).broadcast_to([128, HPG, HD])
                    base = tc * D + l * HD + half * HPG * 4 * HD
                    outv = s_att[:, base: base + HPG * 4 * HD
                                 ].rearrange("p (h w) -> p h w", w=4 * HD)[:, :, 0:HD]
                    e.tensor_tensor(outv, nums, rden, ALU.mult).then_inc(sem_dv.h, 1)
                p.on("vector", fdiv3)
                div_ms.append(sem_dv.bump())
                grp_idx += 1
        div_all = div_ms[-1]

        if stage <= 1:
            def fz(e):
                e.memset(s_xnew[:, :], 0.0).then_inc(sem_dv.h, 1)
            p.on("vector", fz)
            zms = sem_dv.bump()

            def fdone(e, need=div_all, zneed=zms):
                e.wait_ge(sem_dv.h, max(need, zneed))
                e.dma_start(out=xnew_out, in_=s_xnew[:, :]).then_inc(sem_s1.h, 16)
                e.dma_start(out=xn2_out, in_=s_xnew[:, :]).then_inc(sem_s1.h, 16)
                e.dma_start(out=xln_out, in_=s_xnew[:, :]).then_inc(sem_s1.h, 16)
                e.wait_ge(sem_s1.h, 48)
            p.on("sync", fdone)
            sem_s1.bump(48)
            p.play(block)
            return nc

        tr_copy_ms = []
        ti = 0
        for tc in range(NTC):
            for k in range(KC):
                sub = psb[ti % 2][:, 0:128]
                sub_gate = tr_copy_ms[ti - 2] if ti >= 2 else None

                def ftr(e, tc=tc, k=k, sub=sub, sub_gate=sub_gate, need=div_all):
                    e.wait_ge(sem_dv.h, need if sub_gate is None else sub_gate)
                    e.transpose(sub, s_att[:, tc * D + k * 128: tc * D + (k + 1) * 128],
                                s_id[:, :]).then_inc(sem_pe.h, 1)
                p.on("tensor", ftr)
                ms = sem_pe.bump()

                def fcp(e, tc=tc, k=k, sub=sub, ms=ms):
                    e.wait_ge(sem_pe.h, ms)
                    e.tensor_copy(s_acm[:, k * T + tc * 128: k * T + (tc + 1) * 128],
                                  sub).then_inc(sem_dv.h, 1)
                p.on("vector", fcp)
                tr_copy_ms.append(sem_dv.bump())
                ti += 1
        tr_all = tr_copy_ms[-1]

        if stage <= 2:
            def fz(e):
                e.memset(s_xnew[:, :], 0.0).then_inc(sem_dv.h, 1)
            p.on("vector", fz)
            zms = sem_dv.bump()

            def fdone(e, need=tr_all, zneed=zms):
                e.wait_ge(sem_dv.h, max(need, zneed))
                e.dma_start(out=xnew_out, in_=s_xnew[:, :]).then_inc(sem_s1.h, 16)
                e.dma_start(out=xn2_out, in_=s_xnew[:, :]).then_inc(sem_s1.h, 16)
                e.dma_start(out=xln_out, in_=s_xnew[:, :]).then_inc(sem_s1.h, 16)
                e.wait_ge(sem_s1.h, 48)
            p.on("sync", fdone)
            sem_s1.bump(48)
            p.play(block)
            return nc

        def slab_loader(src_tiled, oc, pe_gate):
            sem = sem_w[oc % 2]
            def f(e, src_tiled=src_tiled, oc=oc, pe_gate=pe_gate, sem=sem):
                if pe_gate is not None:
                    e.wait_ge(sem_pe.h, pe_gate)
                e.dma_start(out=s_wslab[oc % 2][:].rearrange("p (k m) -> p k m", m=128),
                            in_=src_tiled[oc].rearrange("k p m -> p k m")
                            ).then_inc(sem.h, 16)
            return f, sem.bump(16)

        comb_ms = []
        c1_ms = []
        proj_oc_end = []
        for oc in range(KC):
            gate = proj_oc_end[oc - 2] if oc >= 2 else None
            f, ready = slab_loader(wp_t, oc, gate)
            p.on("gpsimd", f)
            wsem = sem_w[oc % 2]

            for tt in range(TT):
                g = oc * TT + tt
                bank = ps32[g % 2]
                bank_gate = c1_ms[g - 2] if g >= 2 else None

                ldgate = comb_ms[g - 2] if g >= 2 else None

                xls = sem_xl[g % 2]

                def fxg(e, oc=oc, tt=tt, g=g, ldgate=ldgate, xls=xls):
                    if ldgate is not None:
                        e.wait_ge(sem_dv.h, ldgate)
                    e.dma_start(out=s_xt[g % 2][:],
                                in_=x_cm.rearrange("(c p) t -> p c t", p=128)
                                [:, oc, tt * 512:(tt + 1) * 512]).then_inc(xls.h, 16)
                    e.dma_start(out=s_gt2[g % 2][:],
                                in_=g_in[:, oc * T + tt * 512: oc * T + (tt + 1) * 512]
                                ).then_inc(xls.h, 16)
                p.on("sync", fxg)
                xg_ready = xls.bump(32)

                def fmm(e, oc=oc, tt=tt, bank=bank, ready=ready, wsem=wsem,
                        bank_gate=bank_gate, tr_need=tr_all):
                    e.wait_ge(wsem.h, ready)
                    if oc == 0 and tt == 0:
                        e.wait_ge(sem_dv.h, tr_need)
                    if bank_gate is not None:
                        e.wait_ge(sem_dv.h, bank_gate)
                    ins = None
                    for k in range(KC):
                        ins = e.matmul(bank[:, :],
                                       s_wslab[oc % 2][:, k * 128:(k + 1) * 128],
                                       s_acm[:, k * T + tt * 512: k * T + (tt + 1) * 512],
                                       start=(k == 0), stop=(k == KC - 1))
                    ins.then_inc(sem_pe.h, 1)
                p.on("tensor", fmm)
                pe_ms = sem_pe.bump()

                ptbuf = s_pt[g % 2]

                def fc1(e, oc=oc, bank=bank, pe_ms=pe_ms, ptbuf=ptbuf,
                        war=comb_ms[g - 2] if g >= 2 else None):
                    e.wait_ge(sem_pe.h, pe_ms)
                    if war is not None:
                        e.wait_ge(sem_dv.h, war)
                    e.tensor_scalar_add(ptbuf[:], bank[:, :], s_bp[:, oc:oc + 1]
                                        ).then_inc(sem_dv.h, 1)
                p.on("vector", fc1)
                c1 = sem_dv.bump()
                c1_ms.append(c1)

                def fc2(e, g=g, ptbuf=ptbuf, need=c1, xg=xg_ready, xls=xls,
                        war2=comb_ms[g - 1] if g >= 1 else None):
                    e.wait_ge(sem_dv.h, need)
                    e.wait_ge(xls.h, xg)
                    if war2 is not None:
                        e.wait_ge(sem_dv.h, war2)
                    e.tensor_mul(s_mt[:], ptbuf[:], s_gt2[g % 2][:]).then_inc(sem_dv.h, 1)
                p.on("vector", fc2)
                c2 = sem_dv.bump()

                def fc3(e, oc=oc, tt=tt, g=g, need=c2):
                    e.wait_ge(sem_dv.h, need)
                    e.tensor_add(s_xnew[:, oc * T + tt * 512: oc * T + (tt + 1) * 512],
                                 s_xt[g % 2][:], s_mt[:]).then_inc(sem_dv.h, 1)
                p.on("vector", fc3)
                c3 = sem_dv.bump()
                comb_ms.append(c3)

                def fst1(e, oc=oc, tt=tt, need=c3):
                    e.wait_ge(sem_dv.h, need)
                    e.dma_start(out=xnew_out[:, oc * T + tt * 512: oc * T + (tt + 1) * 512],
                                in_=s_xnew[:, oc * T + tt * 512: oc * T + (tt + 1) * 512]
                                ).then_inc(sem_s1.h, 16)
                p.on("sync", fst1)
                sem_s1.bump(16)
            proj_oc_end.append(sem_pe.n)
        combine_done = comb_ms[-1]

        if stage <= 3:
            def fdone(e, need=combine_done):
                e.wait_ge(sem_dv.h, need)
                e.dma_start(out=xn2_out, in_=s_xnew[:, :]).then_inc(sem_s2.h, 16)
                e.dma_start(out=xln_out, in_=s_xnew[:, :]).then_inc(sem_s2.h, 16)
            p.on("sync", fdone)
            sem_s2.bump(32)
            p.wait("sync", sem_s1, sem_s1.n)
            p.wait("sync", sem_s2, sem_s2.n)
            p.play(block)
            return nc

        p.wait("scalar", sem_dv, combine_done)

        def xn2_emit(e, c, tt, prod_ap, negmr_ap):
            return e.tensor_add(s_xn2[:, c * T + tt * 512: c * T + (tt + 1) * 512],
                                prod_ap, negmr_ap)
        ln1_done, ln1_ms = emit_ln(p, s_xnew, s_rows, s_tmp, s_oc, s_or,
                                   [ps32[2], ps32[3]],
                                   [[ps32[4], ps32[5]], [ps32[0], ps32[1]]],
                                   s_sq + s_sq, sem_pe, sem_dv, sem_ac, KC, T, D,
                                   xn2_emit, eps_ap=s_cst[0:1, 0:1])
        for i, ms in enumerate(ln1_ms):
            c, tt = i // TT, i % TT
            def fst2(e, c=c, tt=tt, need=ms):
                e.wait_ge(sem_dv.h, need)
                e.dma_start(out=xn2_out[:, c * T + tt * 512: c * T + (tt + 1) * 512],
                            in_=s_xn2[:, c * T + tt * 512: c * T + (tt + 1) * 512]
                            ).then_inc(sem_s2.h, 16)
            p.on("sync", fst2)
            sem_s2.bump(16)

        p.wait("scalar", sem_dv, ln1_done)
        dv_base = sem_dv.n
        chain_base = dv_base + 3

        dx_cell = [0]

        def xln_emit(e, c, tt, prod_ap, negmr_ap):
            idx = c * TT + tt
            if idx >= 2:
                e.wait_ge(sem_dv.h, chain_base + 2 * (idx - 2) + 2)
                e.wait_ge(sem_s3[idx % 2].h, 16 * (idx // 2))
            e.tensor_add(s_l1[idx % 2][:], prod_ap, negmr_ap).then_inc(sem_dx.h, 1)
            e.wait_ge(sem_dx.h, idx + 1)
            return e.tensor_scalar(s_l2[idx % 2][:], s_l1[idx % 2][:],
                                   s_gam[:, c:c + 1], s_bet[:, c:c + 1],
                                   ALU.mult, ALU.add)
        ln2_done, ln2_ms = emit_ln(p, s_xn2, s_rows, s_tmp, s_oc, s_or,
                                   [ps32[2], ps32[3]],
                                   [[ps32[4], ps32[5]], [ps32[0], ps32[1]]],
                                   s_sq + s_sq, sem_pe, sem_dv, sem_ac, KC, T, D,
                                   xln_emit, eps_ap=s_cst[0:1, 0:1])
        for i, ms in enumerate(ln2_ms):
            c, tt = i // TT, i % TT
            def fst3(e, c=c, tt=tt, i=i, need=ms):
                e.wait_ge(sem_dv.h, need)
                e.dma_start(out=xln_out[:, c * T + tt * 512: c * T + (tt + 1) * 512],
                            in_=s_l2[i % 2][:]).then_inc(sem_s3[i % 2].h, 16)
            p.on("sync", fst3)
            sem_s3[i % 2].bump(16)

        p.wait("sync", sem_s1, sem_s1.n)
        p.wait("sync", sem_s2, sem_s2.n)
        p.wait("sync", sem_s3[0], sem_s3[0].n)
        p.wait("sync", sem_s3[1], sem_s3[1].n)
        p.play(block)
    return nc



def build_launch_c(D=2048, T=1024, HD=32):
    KC = D // 128
    TT = T // 512

    nc = bass.Bass("TRN2", target_bir_lowering=False, debug=False)
    xln_in = nc.dram_tensor("xln_in", [128, KC * T], F32, kind="ExternalInput").ap()
    bcol = nc.dram_tensor("bcol", [128, KC], F32, kind="ExternalInput").ap()
    mk_pm = nc.dram_tensor("mk_pm", [128, KC], F32, kind="ExternalInput").ap()
    mv_pm = nc.dram_tensor("mv_pm", [128, KC], F32, kind="ExternalInput").ap()
    mr_pm = nc.dram_tensor("mr_pm", [128, KC], F32, kind="ExternalInput").ap()
    td_pm = nc.dram_tensor("td_pm", [128, KC], F32, kind="ExternalInput").ap()
    wk_t = nc.dram_tensor("wk_t", [KC, KC, 128, 128], F32, kind="ExternalInput").ap()
    wv_t = nc.dram_tensor("wv_t", [KC, KC, 128, 128], F32, kind="ExternalInput").ap()
    wr_t = nc.dram_tensor("wr_t", [KC, KC, 128, 128], F32, kind="ExternalInput").ap()

    ek_out = nc.dram_tensor("ek_out", [128, KC * T], F32, kind="ExternalOutput").ap()
    ekv_out = nc.dram_tensor("ekv_out", [128, KC * T], F32, kind="ExternalOutput").ap()
    r_out = nc.dram_tensor("r_out", [128, KC * T], F32, kind="ExternalOutput").ap()
    a_out = nc.dram_tensor("a_out", [128, KC * T], F32, kind="ExternalOutput").ap()
    b_out = nc.dram_tensor("b_out", [128, KC * T], F32, kind="ExternalOutput").ap()
    al_out = nc.dram_tensor("al_out", [128, KC], F32, kind="ExternalOutput").ap()
    bl_out = nc.dram_tensor("bl_out", [128, KC], F32, kind="ExternalOutput").ap()

    from contextlib import ExitStack
    with ExitStack() as ctx:
        e_ = ctx.enter_context
        s_kx = e_(nc.sbuf_tensor("s_kx", [128, KC * T], BF16))
        s_vx = e_(nc.sbuf_tensor("s_vx", [128, KC * T], BF16))
        s_rx = e_(nc.sbuf_tensor("s_rx", [128, KC * T], BF16))
        ar1 = e_(nc.sbuf_tensor("ar1", [128, max(KC * (T + 1), 10 * T)], F32))
        s_xl = ar1.ap()[:, 0:KC * (T + 1)]
        s_ek = [ar1.ap()[:, i * T:(i + 1) * T] for i in range(2)]
        s_ekv = [ar1.ap()[:, (2 + i) * T:(3 + i) * T] for i in range(2)]
        s_r = [ar1.ap()[:, (4 + i) * T:(5 + i) * T] for i in range(2)]
        s_A = [ar1.ap()[:, (6 + i) * T:(7 + i) * T] for i in range(2)]
        s_B = [ar1.ap()[:, (8 + i) * T:(9 + i) * T] for i in range(2)]
        s_wslab = [e_(nc.sbuf_tensor(f"s_wslab{i}", [128, KC * 128], BF16)) for i in range(6)]
        s_d = e_(nc.sbuf_tensor("s_d", [128, T], F32))
        s_mk = e_(nc.sbuf_tensor("s_mk", [128, KC], F32))
        s_mv = e_(nc.sbuf_tensor("s_mv", [128, KC], F32))
        s_mr = e_(nc.sbuf_tensor("s_mr", [128, KC], F32))
        s_td = e_(nc.sbuf_tensor("s_td", [128, KC], F32))
        s_etd = e_(nc.sbuf_tensor("s_etd", [128, KC], F32))
        s_ew = e_(nc.sbuf_tensor("s_ew", [128, KC], F32))
        s_al = e_(nc.sbuf_tensor("s_al", [128, KC], F32))
        s_bl = e_(nc.sbuf_tensor("s_bl", [128, KC], F32))

        psum = [e_(nc.psum_tensor(f"ps{i}", [128, 512], F32)) for i in range(3 * TT)]

        sem_x = Sem(e_(nc.semaphore("sem_x")))
        sem_w = [Sem(e_(nc.semaphore(f"sem_w{i}"))) for i in range(6)]
        sem_pe = Sem(e_(nc.semaphore("sem_pe")))
        sem_dv = Sem(e_(nc.semaphore("sem_dv")))
        sem_ac = Sem(e_(nc.semaphore("sem_ac")))
        sem_so = [Sem(e_(nc.semaphore(f"sem_so{i}"))) for i in range(2)]
        sem_sl = Sem(e_(nc.semaphore("sem_sl")))
        block = e_(nc.Block())

        p = Prog()

        def fload(e):
            e.dma_start(out=s_xl[:].rearrange("p (c t) -> p c t", t=T + 1)[:, :, 1:T + 1],
                        in_=xln_in.rearrange("p (c t) -> p c t", t=T)
                        ).then_inc(sem_x.h, 16)
            e.dma_start(out=s_xl[:].rearrange("p (c t) -> p c t", t=T + 1)[:, :, 0:1],
                        in_=bcol.rearrange("p c -> p c ()"),
                        allow_slow_non_contiguous=True).then_inc(sem_x.h, 16)
            for dst, src in ((s_mk, mk_pm), (s_mv, mv_pm), (s_mr, mr_pm), (s_td, td_pm)):
                e.dma_start(out=dst[:, :], in_=src).then_inc(sem_x.h, 16)
        p.on("sync", fload)
        x_ready = sem_x.bump(16 * 6)

        for eng in ("tensor", "vector", "scalar"):
            p.wait(eng, sem_x, x_ready)

        def few1(e):
            e.activation(s_etd[:, :], s_td[:, :], AF.Exp).then_inc(sem_ac.h, 1)
        p.on("scalar", few1)
        ew1 = sem_ac.bump()

        def few2(e, need=ew1):
            e.wait_ge(sem_ac.h, need)
            e.activation(s_ew[:, :], s_etd[:, :], AF.Exp, scale=-1.0
                         ).then_inc(sem_ac.h, 1)
        p.on("scalar", few2)
        ew_done = sem_ac.bump()

        mix_ms = []
        for c in range(KC):
            war = mix_ms[-1] if mix_ms else None

            def fmix1(e, c=c, war=war):
                if war is not None:
                    e.wait_ge(sem_dv.h, war)
                xc = s_xl[:].rearrange("p (c t) -> p c t", t=T + 1)[:, c, :]
                e.tensor_sub(s_d[:, :], xc[:, 1:T + 1], xc[:, 0:T]
                             ).then_inc(sem_dv.h, 1)
            p.on("vector", fmix1)
            d_ms = sem_dv.bump()

            def fmix2(e, c=c, need=d_ms):
                e.wait_ge(sem_dv.h, need)
                xc = s_xl[:].rearrange("p (c t) -> p c t", t=T + 1)[:, c, :]
                prev = xc[:, 0:T]
                e.scalar_tensor_tensor(s_kx[:, c * T:(c + 1) * T], s_d[:, :],
                                       s_mk[:, c:c + 1], prev, ALU.mult, ALU.add)
                e.scalar_tensor_tensor(s_vx[:, c * T:(c + 1) * T], s_d[:, :],
                                       s_mv[:, c:c + 1], prev, ALU.mult, ALU.add)
                e.scalar_tensor_tensor(s_rx[:, c * T:(c + 1) * T], s_d[:, :],
                                       s_mr[:, c:c + 1], prev, ALU.mult, ALU.add
                                       ).then_inc(sem_dv.h, 1)
            p.on("vector", fmix2)
            mix_ms.append(sem_dv.bump())
        mix_done = mix_ms[-1]

        def slab_loader(src_tiled, oc, slot, pe_gate):
            sem = sem_w[slot]
            def f(e, src_tiled=src_tiled, oc=oc, slot=slot, pe_gate=pe_gate, sem=sem):
                if pe_gate is not None:
                    e.wait_ge(sem_pe.h, pe_gate)
                e.dma_start(out=s_wslab[slot][:].rearrange("p (k m) -> p k m", m=128),
                            in_=src_tiled[oc].rearrange("k p m -> p k m")
                            ).then_inc(sem.h, 16)
            return f, sem.bump(16)

        oc_pe_end = []
        bank_evac = [None] * (3 * TT)
        last_flast = None
        last_r = None

        for oc in range(KC):
            par = oc % 2
            st_gate = 80 * (oc // 2) if oc >= 2 else None
            gate = oc_pe_end[oc - 2] if oc >= 2 else None
            ready = []
            for wi, wt in enumerate((wk_t, wv_t, wr_t)):
                f, rdy = slab_loader(wt, oc, par * 3 + wi, gate)
                p.on("gpsimd", f)
                ready.append(rdy)

            ek_ms = {}
            ekv_ms = {}
            r_ms = {}
            for wi in range(3):
                for tt in range(TT):
                    slot = wi * TT + tt
                    bank = psum[slot]
                    bank_gate = bank_evac[slot]
                    rhs_src = (s_kx, s_vx, s_rx)[wi]

                    def fmm(e, oc=oc, wi=wi, tt=tt, par=par, bank=bank,
                            rdy=ready[wi], wsem=sem_w[par * 3 + wi],
                            bank_gate=bank_gate, rhs_src=rhs_src,
                            first=(oc == 0 and wi == 0 and tt == 0),
                            mixn=mix_done):
                        e.wait_ge(wsem.h, rdy)
                        if first:
                            e.wait_ge(sem_dv.h, mixn)
                        if bank_gate is not None:
                            kind, ms = bank_gate
                            e.wait_ge((sem_ac.h if kind == "ac" else sem_dv.h), ms)
                        ins = None
                        for k in range(KC):
                            ins = e.matmul(bank[:, :],
                                           s_wslab[par * 3 + wi][:, k * 128:(k + 1) * 128],
                                           rhs_src[:, k * T + tt * 512: k * T + (tt + 1) * 512],
                                           start=(k == 0), stop=(k == KC - 1))
                        ins.then_inc(sem_pe.h, 1)
                    p.on("tensor", fmm)
                    pe_ms = sem_pe.bump()

                    if wi == 0:
                        def fev(e, par=par, tt=tt, bank=bank, pe_ms=pe_ms,
                                st_gate=st_gate):
                            e.wait_ge(sem_pe.h, pe_ms)
                            if st_gate is not None:
                                e.wait_ge(sem_so[par].h, st_gate)
                            e.activation(s_ek[par][:, tt * 512:(tt + 1) * 512],
                                         bank[:, :], AF.Exp).then_inc(sem_ac.h, 1)
                        p.on("scalar", fev)
                        ek_ms[tt] = sem_ac.bump()
                        bank_evac[slot] = ("ac", ek_ms[tt])
                    elif wi == 1:
                        def fev(e, par=par, tt=tt, bank=bank, pe_ms=pe_ms,
                                ekn=ek_ms[tt], st_gate=st_gate):
                            e.wait_ge(sem_pe.h, pe_ms)
                            e.wait_ge(sem_ac.h, ekn)
                            if st_gate is not None:
                                e.wait_ge(sem_so[par].h, st_gate)
                            e.tensor_mul(s_ekv[par][:, tt * 512:(tt + 1) * 512],
                                         s_ek[par][:, tt * 512:(tt + 1) * 512],
                                         bank[:, :]).then_inc(sem_dv.h, 1)
                        p.on("vector", fev)
                        ekv_ms[tt] = sem_dv.bump()
                        bank_evac[slot] = ("dv", ekv_ms[tt])
                    else:
                        def fev(e, par=par, tt=tt, bank=bank, pe_ms=pe_ms,
                                st_gate=st_gate):
                            e.wait_ge(sem_pe.h, pe_ms)
                            if st_gate is not None:
                                e.wait_ge(sem_so[par].h, st_gate)
                            e.activation(s_r[par][:, tt * 512:(tt + 1) * 512],
                                         bank[:, :], AF.Sigmoid).then_inc(sem_ac.h, 1)
                        p.on("scalar", fev)
                        r_ms[tt] = sem_ac.bump()
                        bank_evac[slot] = ("ac", r_ms[tt])
            oc_pe_end.append(sem_pe.n)

            def fscanA(e, oc=oc, par=par, need=ekv_ms[TT - 1], ewn=ew_done,
                       st_gate=st_gate):
                e.wait_ge(sem_dv.h, need)
                e.wait_ge(sem_ac.h, ewn)
                if st_gate is not None:
                    e.wait_ge(sem_so[par].h, st_gate)
                e.tensor_tensor_scan(
                    s_A[par][:, :], s_ew[:, oc:oc + 1].broadcast_to([128, T]),
                    s_ekv[par][:, :], 0.0, ALU.mult, ALU.add).then_inc(sem_dv.h, 1)
            p.on("vector", fscanA)
            scanA_ms = sem_dv.bump()

            def fscanB(e, oc=oc, par=par, need=ek_ms[TT - 1], st_gate=st_gate):
                e.wait_ge(sem_ac.h, need)
                if st_gate is not None:
                    e.wait_ge(sem_so[par].h, st_gate)
                e.tensor_tensor_scan(
                    s_B[par][:, :], s_ew[:, oc:oc + 1].broadcast_to([128, T]),
                    s_ek[par][:, :], 0.0, ALU.mult, ALU.add).then_inc(sem_dv.h, 1)
            p.on("vector", fscanB)
            scanB_ms = sem_dv.bump()

            def flast(e, oc=oc, par=par, need=scanB_ms):
                e.wait_ge(sem_dv.h, need)
                e.tensor_copy(s_al[:, oc:oc + 1], s_A[par][:, T - 1:T])
                e.tensor_copy(s_bl[:, oc:oc + 1], s_B[par][:, T - 1:T]
                              ).then_inc(sem_dv.h, 1)
            p.on("vector", flast)
            flast_ms = sem_dv.bump()
            last_flast = flast_ms

            def fst(e, oc=oc, par=par, need=flast_ms, rn=r_ms[TT - 1]):
                e.wait_ge(sem_dv.h, need)
                e.wait_ge(sem_ac.h, rn)
                sl = slice(oc * T, (oc + 1) * T)
                e.dma_start(out=ek_out[:, sl], in_=s_ek[par][:, :]).then_inc(sem_so[par].h, 16)
                e.dma_start(out=ekv_out[:, sl], in_=s_ekv[par][:, :]).then_inc(sem_so[par].h, 16)
                e.dma_start(out=r_out[:, sl], in_=s_r[par][:, :]).then_inc(sem_so[par].h, 16)
                e.dma_start(out=a_out[:, sl], in_=s_A[par][:, :]).then_inc(sem_so[par].h, 16)
                e.dma_start(out=b_out[:, sl], in_=s_B[par][:, :]).then_inc(sem_so[par].h, 16)
            p.on("sync", fst)
            sem_so[par].bump(80)

        def flaststore(e, need=last_flast):
            e.wait_ge(sem_dv.h, need)
            e.dma_start(out=al_out, in_=s_al[:, :]).then_inc(sem_sl.h, 16)
            e.dma_start(out=bl_out, in_=s_bl[:, :]).then_inc(sem_sl.h, 16)
        p.on("sync", flaststore)
        sem_sl.bump(32)

        p.wait("sync", sem_so[0], sem_so[0].n)
        p.wait("sync", sem_so[1], sem_so[1].n)
        p.wait("sync", sem_sl, sem_sl.n)
        p.play(block)
    return nc



def build_launch_d1(D=2048, T=1024, HD=32):
    KC = D // 128
    TT = T // 512

    nc = bass.Bass("TRN2", target_bir_lowering=False, debug=False)
    a_in = nc.dram_tensor("a_in", [128, KC * T], F32, kind="ExternalInput").ap()
    b_in = nc.dram_tensor("b_in", [128, KC * T], F32, kind="ExternalInput").ap()
    ek_in = nc.dram_tensor("ek_in", [128, KC * T], F32, kind="ExternalInput").ap()
    ekv_in = nc.dram_tensor("ekv_in", [128, KC * T], F32, kind="ExternalInput").ap()
    r_in = nc.dram_tensor("r_in", [128, KC * T], F32, kind="ExternalInput").ap()
    xn2_in = nc.dram_tensor("xn2_in", [128, KC * T], F32, kind="ExternalInput").ap()
    acar = nc.dram_tensor("acar", [128, KC], F32, kind="ExternalInput").ap()
    bcar = nc.dram_tensor("bcar", [128, KC], F32, kind="ExternalInput").ap()
    tf_pm = nc.dram_tensor("tf_pm", [128, KC], F32, kind="ExternalInput").ap()
    td_row = nc.dram_tensor("td_row", [1, D], F32, kind="ExternalInput").ap()
    iota_row = nc.dram_tensor("iota_row", [1, T], F32, kind="ExternalInput").ap()
    wo_t = nc.dram_tensor("wo_t", [KC, KC, 128, 128], F32, kind="ExternalInput").ap()
    cst_col = nc.dram_tensor("cst_col", [128, 2], F32, kind="ExternalInput").ap()

    s_out = nc.dram_tensor("s_out", [128, KC * T], F32, kind="ExternalOutput").ap()

    from contextlib import ExitStack
    with ExitStack() as ctx:
        e_ = ctx.enter_context
        s_rwkv = e_(nc.sbuf_tensor("s_rwkv", [128, KC * T], BF16))
        s_As = [e_(nc.sbuf_tensor(f"s_As{i}", [128, T + 1], F32)) for i in range(2)]
        s_Bs = [e_(nc.sbuf_tensor(f"s_Bs{i}", [128, T + 1], F32)) for i in range(2)]
        s_eks = [e_(nc.sbuf_tensor(f"s_eks{i}", [128, T], F32)) for i in range(2)]
        s_ekvs = [e_(nc.sbuf_tensor(f"s_ekvs{i}", [128, T], F32)) for i in range(2)]
        s_rs = [e_(nc.sbuf_tensor(f"s_rs{i}", [128, T], F32)) for i in range(2)]
        s_dp = [e_(nc.sbuf_tensor(f"s_dp{i}", [128, T], F32)) for i in range(2)]
        s_t1 = e_(nc.sbuf_tensor("s_t1", [128, T], F32))
        s_num = e_(nc.sbuf_tensor("s_num", [128, T], F32))
        s_den = e_(nc.sbuf_tensor("s_den", [128, T], F32))
        s_ld = e_(nc.sbuf_tensor("s_ld", [128, T], F32))
        s_rd = e_(nc.sbuf_tensor("s_rd", [128, T], F32))
        s_ac = e_(nc.sbuf_tensor("s_ac", [128, KC], F32))
        s_bc = e_(nc.sbuf_tensor("s_bc", [128, KC], F32))
        s_tf = e_(nc.sbuf_tensor("s_tf", [128, KC], F32))
        s_eu = e_(nc.sbuf_tensor("s_eu", [128, KC], F32))
        s_tdr = e_(nc.sbuf_tensor("s_tdr", [1, D], F32))
        s_etdr = e_(nc.sbuf_tensor("s_etdr", [1, D], F32))
        s_wrow = e_(nc.sbuf_tensor("s_wrow", [1, D], F32))
        s_io = e_(nc.sbuf_tensor("s_io", [1, T], F32))
        s_cc = e_(nc.sbuf_tensor("s_cc", [128, 2], F32))
        s_wslab = [e_(nc.sbuf_tensor(f"s_wslab{i}", [128, KC * 128], BF16)) for i in range(2)]
        s_x2t = [e_(nc.sbuf_tensor(f"s_x2t{i}", [128, 512], F32)) for i in range(2)]
        s_st = [e_(nc.sbuf_tensor(f"s_st{i}", [128, 512], F32)) for i in range(2)]

        psum = [e_(nc.psum_tensor(f"ps{i}", [128, 512], F32)) for i in range(6)]

        sem_x = Sem(e_(nc.semaphore("sem_x")))
        sem_ld = [Sem(e_(nc.semaphore(f"sem_ld{i}"))) for i in range(2)]
        sem_w = [Sem(e_(nc.semaphore(f"sem_w{i}"))) for i in range(2)]
        sem_x2 = [Sem(e_(nc.semaphore(f"sem_x2{i}"))) for i in range(2)]
        sem_pe = Sem(e_(nc.semaphore("sem_pe")))
        sem_dv = Sem(e_(nc.semaphore("sem_dv")))
        sem_ac = Sem(e_(nc.semaphore("sem_ac")))
        sem_st = [Sem(e_(nc.semaphore(f"sem_st{i}"))) for i in range(2)]
        block = e_(nc.Block())

        p = Prog()

        def fload(e):
            for dst, src in ((s_ac, acar), (s_bc, bcar), (s_tf, tf_pm),
                             (s_cc, cst_col)):
                e.dma_start(out=dst[:, :], in_=src).then_inc(sem_x.h, 16)
            e.dma_start(out=s_tdr[:, :], in_=td_row).then_inc(sem_x.h, 16)
            e.dma_start(out=s_io[:, :], in_=iota_row).then_inc(sem_x.h, 16)
        p.on("sync", fload)
        x_ready = sem_x.bump(16 * 6)

        for eng in ("tensor", "vector", "scalar"):
            p.wait(eng, sem_x, x_ready)

        def fzero(e):
            e.memset(s_As[0][:, 0:1], 0.0)
            e.memset(s_As[1][:, 0:1], 0.0)
            e.memset(s_Bs[0][:, 0:1], 0.0)
            e.memset(s_Bs[1][:, 0:1], 0.0).then_inc(sem_dv.h, 1)
        p.on("vector", fzero)
        zero_done = sem_dv.bump()

        def feu(e):
            e.activation(s_eu[:, :], s_tf[:, :], AF.Exp)
            e.activation(s_etdr[:, :], s_tdr[:, :], AF.Exp).then_inc(sem_ac.h, 1)
        p.on("scalar", feu)
        eu_done = sem_ac.bump()

        def fwrow(e, need=eu_done):
            e.wait_ge(sem_ac.h, need)
            e.tensor_scalar_mul(s_wrow[:, :], s_etdr[:, :], -1.0).then_inc(sem_dv.h, 1)
        p.on("vector", fwrow)
        wrow_done = sem_dv.bump()

        rwkv_ms = []
        for c in range(KC):
            par = c % 2
            ld_gate = rwkv_ms[c - 2] if c >= 2 else None

            def fldc(e, c=c, par=par, ld_gate=ld_gate, z=zero_done):
                if ld_gate is not None:
                    e.wait_ge(sem_dv.h, ld_gate)
                else:
                    e.wait_ge(sem_dv.h, z)
                sl = slice(c * T, (c + 1) * T)
                e.dma_start(out=s_As[par][:, 1:T + 1], in_=a_in[:, sl]).then_inc(sem_ld[par].h, 16)
                e.dma_start(out=s_Bs[par][:, 1:T + 1], in_=b_in[:, sl]).then_inc(sem_ld[par].h, 16)
                e.dma_start(out=s_eks[par][:, :], in_=ek_in[:, sl]).then_inc(sem_ld[par].h, 16)
                e.dma_start(out=s_ekvs[par][:, :], in_=ekv_in[:, sl]).then_inc(sem_ld[par].h, 16)
                e.dma_start(out=s_rs[par][:, :], in_=r_in[:, sl]).then_inc(sem_ld[par].h, 16)
            p.on("sync", fldc)
            ld_ready = sem_ld[par].bump(80)

            def fdp_mm(e, c=c, par=par, need=wrow_done,
                       bank_gate=rwkv_ms[c - 2] if c >= 2 else None):
                e.wait_ge(sem_dv.h, need if bank_gate is None else max(need, bank_gate))
                ins = None
                for tt in range(TT):
                    ins = e.matmul(psum[par * 2 + tt][:, :],
                                   s_wrow[0:1, c * 128:(c + 1) * 128],
                                   s_io[0:1, tt * 512:(tt + 1) * 512],
                                   start=True, stop=True)
                ins.then_inc(sem_pe.h, 1)
            p.on("tensor", fdp_mm)
            dp_pe = sem_pe.bump()

            def fdp_exp(e, c=c, par=par, need=dp_pe,
                        war=rwkv_ms[c - 2] if c >= 2 else None):
                e.wait_ge(sem_pe.h, need)
                if war is not None:
                    e.wait_ge(sem_dv.h, war)
                ins = None
                for tt in range(TT):
                    ins = e.activation(s_dp[par][:, tt * 512:(tt + 1) * 512],
                                       psum[par * 2 + tt][:, :], AF.Exp)
                ins.then_inc(sem_ac.h, 1)
            p.on("scalar", fdp_exp)
            dp_ac = sem_ac.bump()

            prev_chain = rwkv_ms[-1] if rwkv_ms else None

            def fchain1(e, c=c, par=par, dp=dp_ac, ld=ld_ready, prev=prev_chain):
                e.wait_ge(sem_ac.h, dp)
                e.wait_ge(sem_ld[par].h, ld)
                if prev is not None:
                    e.wait_ge(sem_dv.h, prev)
                e.scalar_tensor_tensor(s_t1[:, :], s_dp[par][:, :], s_ac[:, c:c + 1],
                                       s_As[par][:, 0:T], ALU.mult, ALU.add
                                       ).then_inc(sem_dv.h, 1)
            p.on("vector", fchain1)
            m1 = sem_dv.bump()

            def fchain2a(e, c=c, par=par, need=m1):
                e.wait_ge(sem_dv.h, need)
                e.scalar_tensor_tensor(s_num[:, :], s_ekvs[par][:, :], s_eu[:, c:c + 1],
                                       s_t1[:, :], ALU.mult, ALU.add
                                       ).then_inc(sem_dv.h, 1)
            p.on("vector", fchain2a)
            m2a = sem_dv.bump()

            def fchain2(e, c=c, par=par, need=m2a):
                e.wait_ge(sem_dv.h, need)
                e.scalar_tensor_tensor(s_t1[:, :], s_dp[par][:, :], s_bc[:, c:c + 1],
                                       s_Bs[par][:, 0:T], ALU.mult, ALU.add
                                       ).then_inc(sem_dv.h, 1)
            p.on("vector", fchain2)
            m2 = sem_dv.bump()

            def fchain3(e, c=c, par=par, need=m2):
                e.wait_ge(sem_dv.h, need)
                e.scalar_tensor_tensor(s_den[:, :], s_eks[par][:, :], s_eu[:, c:c + 1],
                                       s_t1[:, :], ALU.mult, ALU.add
                                       ).then_inc(sem_dv.h, 1)
            p.on("vector", fchain3)
            m3 = sem_dv.bump()

            def fln(e, need=m3):
                e.wait_ge(sem_dv.h, need)
                e.activation(s_ld[:, :], s_den[:, :], AF.Ln, bias=s_cc[:, 1:2]
                             ).then_inc(sem_ac.h, 1)
            p.on("scalar", fln)
            m4 = sem_ac.bump()

            def fexp(e, need=m4):
                e.wait_ge(sem_ac.h, need)
                e.activation(s_rd[:, :], s_ld[:, :], AF.Exp, scale=-1.0
                             ).then_inc(sem_ac.h, 1)
            p.on("scalar", fexp)
            m5 = sem_ac.bump()

            def fchain4(e, c=c, par=par, need=m5):
                e.wait_ge(sem_ac.h, need)
                e.tensor_mul(s_t1[:, :], s_num[:, :], s_rd[:, :]).then_inc(sem_dv.h, 1)
            p.on("vector", fchain4)
            m6 = sem_dv.bump()

            def fchain5(e, c=c, par=par, need=m6):
                e.wait_ge(sem_dv.h, need)
                e.tensor_mul(s_rwkv[:, c * T:(c + 1) * T], s_t1[:, :], s_rs[par][:, :]
                             ).then_inc(sem_dv.h, 1)
            p.on("vector", fchain5)
            rwkv_ms.append(sem_dv.bump())
        rwkv_all = rwkv_ms[-1]

        def slab_loader(src_tiled, oc, pe_gate):
            sem = sem_w[oc % 2]
            def f(e, src_tiled=src_tiled, oc=oc, pe_gate=pe_gate, sem=sem):
                if pe_gate is not None:
                    e.wait_ge(sem_pe.h, pe_gate)
                e.dma_start(out=s_wslab[oc % 2][:].rearrange("p (k m) -> p k m", m=128),
                            in_=src_tiled[oc].rearrange("k p m -> p k m")
                            ).then_inc(sem.h, 16)
            return f, sem.bump(16)

        ev_ms = []
        oc_end = []
        for oc in range(KC):
            gate = oc_end[oc - 2] if oc >= 2 else None
            f, ready = slab_loader(wo_t, oc, gate)
            p.on("gpsimd", f)
            wsem = sem_w[oc % 2]

            for tt in range(TT):
                g = oc * TT + tt
                bank = psum[4 + g % 2]
                bank_gate = ev_ms[g - 2] if g >= 2 else None
                x2s = sem_x2[g % 2]

                def fx2(e, oc=oc, tt=tt, g=g, x2s=x2s,
                        ldg=ev_ms[g - 2] if g >= 2 else None):
                    if ldg is not None:
                        e.wait_ge(sem_dv.h, ldg)
                    e.dma_start(out=s_x2t[g % 2][:],
                                in_=xn2_in[:, oc * T + tt * 512: oc * T + (tt + 1) * 512]
                                ).then_inc(x2s.h, 16)
                p.on("sync", fx2)
                x2_ready = x2s.bump(16)

                def fmm(e, oc=oc, tt=tt, bank=bank, ready=ready, wsem=wsem,
                        bank_gate=bank_gate, rall=rwkv_all):
                    e.wait_ge(wsem.h, ready)
                    if oc == 0 and tt == 0:
                        e.wait_ge(sem_dv.h, rall)
                    if bank_gate is not None:
                        e.wait_ge(sem_dv.h, bank_gate)
                    ins = None
                    for k in range(KC):
                        ins = e.matmul(bank[:, :],
                                       s_wslab[oc % 2][:, k * 128:(k + 1) * 128],
                                       s_rwkv[:, k * T + tt * 512: k * T + (tt + 1) * 512],
                                       start=(k == 0), stop=(k == KC - 1))
                    ins.then_inc(sem_pe.h, 1)
                p.on("tensor", fmm)
                pe_ms = sem_pe.bump()

                def fev(e, oc=oc, tt=tt, g=g, bank=bank, pe_ms=pe_ms,
                        x2n=x2_ready, x2s=x2s,
                        stw=16 * (g // 2) if g >= 2 else None):
                    e.wait_ge(sem_pe.h, pe_ms)
                    e.wait_ge(x2s.h, x2n)
                    if stw is not None:
                        e.wait_ge(sem_st[g % 2].h, stw)
                    e.tensor_add(s_st[g % 2][:], bank[:, :], s_x2t[g % 2][:]
                                 ).then_inc(sem_dv.h, 1)
                p.on("vector", fev)
                ev = sem_dv.bump()
                ev_ms.append(ev)

                def fst(e, oc=oc, tt=tt, g=g, need=ev):
                    e.wait_ge(sem_dv.h, need)
                    e.dma_start(out=s_out[:, oc * T + tt * 512: oc * T + (tt + 1) * 512],
                                in_=s_st[g % 2][:]).then_inc(sem_st[g % 2].h, 16)
                p.on("sync", fst)
                sem_st[g % 2].bump(16)
            oc_end.append(sem_pe.n)

        p.wait("sync", sem_st[0], sem_st[0].n)
        p.wait("sync", sem_st[1], sem_st[1].n)
        p.play(block)
    return nc



def build_launch_d2(D=2048, T=1024, HD=32, NE=32, NG=8, I3=768):
    KC = D // 128
    TT = T // 512
    NP = TT
    NTC = 512 // 128
    IC = I3 // 128
    OCU = 2 * I3 // 128

    nc = bass.Bass("TRN2", target_bir_lowering=False, debug=False)
    xnew_in = nc.dram_tensor("xnew_in", [128, KC * T], F32, kind="ExternalInput").ap()
    s_in = nc.dram_tensor("s_in", [128, KC * T], F32, kind="ExternalInput").ap()
    xn2_in = nc.dram_tensor("xn2_in", [128, KC * T], F32, kind="ExternalInput").ap()
    gw_t = nc.dram_tensor("gw_t", [KC, 128, NE], F32, kind="ExternalInput").ap()
    gu_t = nc.dram_tensor("gu_t", [NE, OCU, 128, KC, 128], F32, kind="ExternalInput").ap()
    dn_t = nc.dram_tensor("dn_t", [NE, KC, 128, IC, 128], F32, kind="ExternalInput").ap()
    mg_i = nc.dram_tensor("mg_i", [NE, 1], I32, kind="ExternalInput").ap()
    de_i = nc.dram_tensor("de_i", [1, NG], I32, kind="ExternalInput").ap()
    io8 = nc.dram_tensor("io8", [1, NG], F32, kind="ExternalInput").ap()
    ident = nc.dram_tensor("ident", [128, 128], BF16, kind="ExternalInput").ap()
    ones_row = nc.dram_tensor("ones_row", [1, 128], F32, kind="ExternalInput").ap()

    out_t = nc.dram_tensor("out", [128, KC * T], F32, kind="ExternalOutput").ap()
    wtg_dram = nc.dram_tensor("wtg_dram", [NG, 512], F32, kind="Internal").ap()
    sg_gu = nc.dram_tensor("sg_gu", [NG, OCU, 128, KC, 128], BF16, kind="Internal").ap()
    sg_dn = nc.dram_tensor("sg_dn", [NG, KC, 128, IC, 128], BF16, kind="Internal").ap()

    from contextlib import ExitStack
    with ExitStack() as ctx:
        e_ = ctx.enter_context
        s_x2b = e_(nc.sbuf_tensor("s_x2b", [128, KC * 512], BF16))
        s_act = [e_(nc.sbuf_tensor(f"s_act{g}", [128, IC * 512], BF16)) for g in range(NG)]
        s_gus = [e_(nc.sbuf_tensor(f"s_gus{i}", [128, 2 * KC * 128], BF16)) for i in range(2)]
        s_dns = [e_(nc.sbuf_tensor(f"s_dns{i}", [128, NG * IC * 128], BF16)) for i in range(2)]
        s_gw = e_(nc.sbuf_tensor("s_gw", [128, KC * NE], BF16))
        s_lg0 = e_(nc.sbuf_tensor("s_lg0", [128, NTC * NE], F32))
        s_lgp = [e_(nc.sbuf_tensor(f"s_lgp{i}", [128, NTC * NE], F32)) for i in range(2)]
        s_selp = [e_(nc.sbuf_tensor(f"s_selp{i}", [128, NTC * NE], F32)) for i in range(2)]
        s_msk = e_(nc.sbuf_tensor("s_msk", [128, NTC * NE], F32))
        s_eb = e_(nc.sbuf_tensor("s_eb", [128, NTC * NE], F32))
        s_ps = e_(nc.sbuf_tensor("s_ps", [128, NTC * NE], F32))
        s_we = e_(nc.sbuf_tensor("s_we", [128, NTC * NE], BF16))
        s_m = e_(nc.sbuf_tensor("s_m", [128, NTC], F32))
        s_m1 = e_(nc.sbuf_tensor("s_m1", [128, NTC], F32))
        s_l8 = e_(nc.sbuf_tensor("s_l8", [128, NTC], F32))
        s_t8 = e_(nc.sbuf_tensor("s_t8", [128, NTC], F32))
        s_rn = e_(nc.sbuf_tensor("s_rn", [128, NTC], F32))
        s_weT = e_(nc.sbuf_tensor("s_weT", [32, NTC * 128], BF16))
        s_mg = e_(nc.sbuf_tensor("s_mg", [NE, 1], I32))
        s_mgf = e_(nc.sbuf_tensor("s_mgf", [NE, 1], F32))
        s_oh = e_(nc.sbuf_tensor("s_oh", [NE, NG], BF16))
        s_de = e_(nc.sbuf_tensor("s_de", [1, NG], I32))
        s_io8 = e_(nc.sbuf_tensor("s_io8", [1, NG], F32))
        s_id = e_(nc.sbuf_tensor("s_id", [128, 128], BF16))
        s_or = e_(nc.sbuf_tensor("s_or", [1, 128], F32))
        s_wtg8 = e_(nc.sbuf_tensor("s_wtg8", [NG, 512], F32))
        s_wtgr = e_(nc.sbuf_tensor("s_wtgr", [1, NG * 512], F32))
        s_wtb = e_(nc.sbuf_tensor("s_wtb", [128, NG * 512], F32))
        s_sg = e_(nc.sbuf_tensor("s_sg", [128, 512], F32))
        s_t = e_(nc.sbuf_tensor("s_t", [128, 512], F32))
        s_g2 = e_(nc.sbuf_tensor("s_g2", [128, 512], F32))
        s_stl = [e_(nc.sbuf_tensor(f"s_stl{i}", [128, 512], F32)) for i in range(2)]
        s_xnt = [e_(nc.sbuf_tensor(f"s_xnt{i}", [128, 512], F32)) for i in range(2)]
        s_ot = [e_(nc.sbuf_tensor(f"s_ot{i}", [128, 512], F32)) for i in range(2)]

        ps32 = [e_(nc.psum_tensor(f"ps{i}", [128, 512], F32)) for i in range(6)]
        psb = [e_(nc.psum_tensor(f"psb{i}", [128, 1024], BF16)) for i in range(2)]

        sem_x = Sem(e_(nc.semaphore("sem_x")))
        sem_gu = [Sem(e_(nc.semaphore(f"sem_gu{i}"))) for i in range(2)]
        sem_dn = [Sem(e_(nc.semaphore(f"sem_dn{i}"))) for i in range(2)]
        sem_sx = [Sem(e_(nc.semaphore(f"sem_sx{i}"))) for i in range(2)]
        sem_rg = Sem(e_(nc.semaphore("sem_rg")))
        sem_pe = Sem(e_(nc.semaphore("sem_pe")))
        sem_dv = Sem(e_(nc.semaphore("sem_dv")))
        sem_ac = Sem(e_(nc.semaphore("sem_ac")))
        sem_wt = Sem(e_(nc.semaphore("sem_wt")))
        sem_gwl = Sem(e_(nc.semaphore("sem_gwl")))
        sem_x2 = Sem(e_(nc.semaphore("sem_x2")))
        sem_st = [Sem(e_(nc.semaphore(f"sem_st{i}"))) for i in range(2)]
        block = e_(nc.Block())

        p = Prog()

        def fload(e):
            for dst, src in ((s_mg[:, :], mg_i), (s_de[:, :], de_i),
                             (s_io8[:, :], io8), (s_id[:, :], ident),
                             (s_or[:, :], ones_row)):
                e.dma_start(out=dst, in_=src).then_inc(sem_x.h, 16)
        p.on("sync", fload)

        def floadg(e):
            e.dma_start(out=s_gw[:].rearrange("p (k e) -> p k e", e=NE),
                        in_=gw_t.rearrange("k p e -> p k e")).then_inc(sem_gwl.h, 16)
        p.on("gpsimd", floadg)
        x_ready = sem_x.bump(16 * 5)
        gw_ready = sem_gwl.bump(16)

        for eng in ("tensor", "vector", "scalar", "gpsimd"):
            p.wait(eng, sem_x, x_ready)
            p.wait(eng, sem_gwl, gw_ready)

        def fregs(e):
            for g in range(NG):
                rg = e.alloc_register(f"de{g}")
                e.load(rg, s_de[0:1, g:g + 1])
                rv = bass.RuntimeValue(val=rg, min_val=0, max_val=NE - 1)
                e.dma_start(out=sg_gu[g], in_=gu_t[rv]).then_inc(sem_rg.h, 16)
                e.dma_start(out=sg_dn[g], in_=dn_t[rv]).then_inc(sem_rg.h, 16)
        p.on("gpsimd", fregs)
        regs_done = sem_rg.bump(16 * 2 * NG)

        def fmgf(e):
            e.tensor_copy(s_mgf[:, :], s_mg[:, :]).then_inc(sem_dv.h, 1)
        p.on("vector", fmgf)
        mgf_done = sem_dv.bump()

        def fiob(e):
            e.matmul(ps32[1][0:NE, 0:NG], s_or[0:1, 0:NE], s_io8[:, :],
                     start=True, stop=True).then_inc(sem_pe.h, 1)
        p.on("tensor", fiob)
        iob_done = sem_pe.bump()

        def foh(e, n1=mgf_done, n2=iob_done):
            e.wait_ge(sem_dv.h, n1)
            e.wait_ge(sem_pe.h, n2)
            e.tensor_tensor(s_oh[:, :], ps32[1][0:NE, 0:NG],
                            s_mgf[:, :].broadcast_to([NE, NG]), ALU.is_equal
                            ).then_inc(sem_dv.h, 1)
        p.on("vector", foh)
        oh_done = sem_dv.bump()

        BIG = 60000.0

        for ph in range(NP):
            if ph > 0:
                for eng in ("tensor", "vector", "scalar", "gpsimd", "sync"):
                    p.wait(eng, sem_st[0], sem_st[0].n)
                    p.wait(eng, sem_st[1], sem_st[1].n)

            def fx2b(e, ph=ph):
                e.dma_start(
                    out=s_x2b[:].rearrange("p (c t) -> p c t", t=512),
                    in_=xn2_in.rearrange("p (c t) -> p c t", t=T)
                    [:, :, ph * 512:(ph + 1) * 512]).then_inc(sem_x2.h, 16)
            p.on("gpsimd", fx2b)
            x2b_ready = sem_x2.bump(16)

            def flg(e, x2n=x2b_ready, ohn=oh_done):
                e.wait_ge(sem_x2.h, x2n)
                e.wait_ge(sem_dv.h, ohn)
                ins = None
                for tcc in range(NTC):
                    for k in range(KC):
                        ins = e.matmul(ps32[0][:, tcc * NE:(tcc + 1) * NE],
                                       s_x2b[:, k * 512 + tcc * 128: k * 512 + (tcc + 1) * 128],
                                       s_gw[:, k * NE:(k + 1) * NE],
                                       start=(k == 0), stop=(k == KC - 1))
                ins.then_inc(sem_pe.h, 1)
            p.on("tensor", flg)
            lg_pe = sem_pe.bump()

            def flg_ev(e, need=lg_pe):
                e.wait_ge(sem_pe.h, need)
                e.tensor_copy(s_lg0[:, :], ps32[0][:, 0:NTC * NE])
                e.tensor_copy(s_lgp[0][:, :], ps32[0][:, 0:NTC * NE])
                e.memset(s_selp[0][:, :], 0.0).then_inc(sem_dv.h, 1)
            p.on("vector", flg_ev)
            prev_dv = sem_dv.bump()

            cur = 0
            for it in range(NG):
                def fit_a(e, it=it, cl=cur, need=prev_dv):
                    e.wait_ge(sem_dv.h, need)
                    lgv = s_lgp[cl][:, :].rearrange("p (c e) -> p c e", e=NE)
                    e.reduce_max(s_m[:, :].rearrange("p c -> p c ()"), lgv, axis=AX.X
                                 ).then_inc(sem_dv.h, 1)
                p.on("vector", fit_a)
                ma = sem_dv.bump()

                def fit_b(e, it=it, cl=cur, need=ma):
                    e.wait_ge(sem_dv.h, need)
                    if it == 0:
                        e.tensor_copy(s_m1[:, :], s_m[:, :])
                    lgv = s_lgp[cl][:, :].rearrange("p (c e) -> p c e", e=NE)
                    e.tensor_tensor(s_msk[:, :].rearrange("p (c e) -> p c e", e=NE),
                                    lgv,
                                    s_m[:, :].rearrange("p c -> p c ()"
                                                        ).broadcast_to([128, NTC, NE]),
                                    ALU.is_equal).then_inc(sem_dv.h, 1)
                p.on("vector", fit_b)
                mb = sem_dv.bump()

                def fit_c(e, cl=cur, need=mb):
                    e.wait_ge(sem_dv.h, need)
                    e.tensor_add(s_selp[1 - cl][:, :], s_selp[cl][:, :], s_msk[:, :])
                    e.scalar_tensor_tensor(s_lgp[1 - cl][:, :], s_msk[:, :], -BIG,
                                           s_lgp[cl][:, :], ALU.mult, ALU.add
                                           ).then_inc(sem_dv.h, 1)
                p.on("vector", fit_c)
                prev_dv = sem_dv.bump()
                cur = 1 - cur
            sel_final = cur

            def fsub(e, need=prev_dv):
                e.wait_ge(sem_dv.h, need)
                e.tensor_tensor(s_msk[:, :].rearrange("p (c e) -> p c e", e=NE),
                                s_lg0[:, :].rearrange("p (c e) -> p c e", e=NE),
                                s_m1[:, :].rearrange("p c -> p c ()"
                                                     ).broadcast_to([128, NTC, NE]),
                                ALU.subtract).then_inc(sem_dv.h, 1)
            p.on("vector", fsub)
            sub_ms = sem_dv.bump()

            def febx(e, need=sub_ms):
                e.wait_ge(sem_dv.h, need)
                e.activation(s_eb[:, :], s_msk[:, :], AF.Exp).then_inc(sem_ac.h, 1)
            p.on("scalar", febx)
            eb_ms = sem_ac.bump()

            def fps(e, need=eb_ms, cs=sel_final):
                e.wait_ge(sem_ac.h, need)
                e.tensor_mul(s_ps[:, :], s_eb[:, :], s_selp[cs][:, :]
                             ).then_inc(sem_dv.h, 1)
            p.on("vector", fps)
            ps_ms = sem_dv.bump()

            def fdn8(e, need=ps_ms):
                e.wait_ge(sem_dv.h, need)
                e.reduce_sum(s_l8[:, :].rearrange("p c -> p c ()"),
                             s_ps[:, :].rearrange("p (c e) -> p c e", e=NE), axis=AX.X
                             ).then_inc(sem_dv.h, 1)
            p.on("vector", fdn8)
            dn8_ms = sem_dv.bump()

            def fln8(e, need=dn8_ms):
                e.wait_ge(sem_dv.h, need)
                e.activation(s_t8[:, :], s_l8[:, :], AF.Ln).then_inc(sem_ac.h, 1)
            p.on("scalar", fln8)
            ln8_ms = sem_ac.bump()

            def fexp8(e, need=ln8_ms):
                e.wait_ge(sem_ac.h, need)
                e.activation(s_rn[:, :], s_t8[:, :], AF.Exp, scale=-1.0
                             ).then_inc(sem_ac.h, 1)
            p.on("scalar", fexp8)
            rn_ms = sem_ac.bump()

            def fwe(e, need=rn_ms):
                e.wait_ge(sem_ac.h, need)
                e.tensor_tensor(s_we[:, :].rearrange("p (c e) -> p c e", e=NE),
                                s_ps[:, :].rearrange("p (c e) -> p c e", e=NE),
                                s_rn[:, :].rearrange("p c -> p c ()"
                                                     ).broadcast_to([128, NTC, NE]),
                                ALU.mult).then_inc(sem_dv.h, 1)
            p.on("vector", fwe)
            we_ms = sem_dv.bump()

            tr_ms = []
            for tcc in range(NTC):
                def ftr(e, tcc=tcc, need=we_ms,
                        gate=tr_ms[tcc - 2] if tcc >= 2 else None):
                    e.wait_ge(sem_dv.h, need if gate is None else gate)
                    e.transpose(psb[tcc % 2][0:NE, 0:128],
                                s_we[:, tcc * NE:(tcc + 1) * NE], s_id[:, :]
                                ).then_inc(sem_pe.h, 1)
                p.on("tensor", ftr)
                pe_ms = sem_pe.bump()

                def ftrc(e, tcc=tcc, need=pe_ms):
                    e.wait_ge(sem_pe.h, need)
                    e.tensor_copy(s_weT[:, tcc * 128:(tcc + 1) * 128],
                                  psb[tcc % 2][0:NE, 0:128]).then_inc(sem_dv.h, 1)
                p.on("vector", ftrc)
                tr_ms.append(sem_dv.bump())

            def fwtg(e, need=tr_ms[-1], ohn=oh_done):
                e.wait_ge(sem_dv.h, max(need, ohn))
                e.matmul(ps32[1][0:NG, 0:512], s_oh[:, :], s_weT[:, :],
                         start=True, stop=True).then_inc(sem_pe.h, 1)
            p.on("tensor", fwtg)
            wtg_pe = sem_pe.bump()

            def fwtg_ev(e, need=wtg_pe):
                e.wait_ge(sem_pe.h, need)
                e.tensor_copy(s_wtg8[:, :], ps32[1][0:NG, 0:512]).then_inc(sem_dv.h, 1)
            p.on("vector", fwtg_ev)
            wtg_ev = sem_dv.bump()

            def fbounce(e, need=wtg_ev, base=sem_wt.n):
                e.wait_ge(sem_dv.h, need)
                e.dma_start(out=wtg_dram, in_=s_wtg8[:, :]).then_inc(sem_wt.h, 16)
                e.wait_ge(sem_wt.h, base + 16)
                e.dma_start(out=s_wtgr[:, :],
                            in_=wtg_dram.rearrange("g t -> () (g t)")
                            ).then_inc(sem_wt.h, 16)
            p.on("sync", fbounce)
            wtgr_ready = sem_wt.bump(32)

            bc_ms = []
            for g in range(NG):
                def fbc(e, g=g, need=wtgr_ready,
                        gate=bc_ms[g - 1] if g >= 1 else None):
                    e.wait_ge(sem_wt.h, need)
                    if gate is not None:
                        e.wait_ge(sem_dv.h, gate)
                    e.matmul(ps32[2][:, :], s_or[0:1, 0:128],
                             s_wtgr[0:1, g * 512:(g + 1) * 512],
                             start=True, stop=True).then_inc(sem_pe.h, 1)
                p.on("tensor", fbc)
                pe_ms = sem_pe.bump()

                def fbcev(e, g=g, need=pe_ms):
                    e.wait_ge(sem_pe.h, need)
                    e.tensor_copy(s_wtb[:, g * 512:(g + 1) * 512], ps32[2][:, :]
                                  ).then_inc(sem_dv.h, 1)
                p.on("vector", fbcev)
                bc_ms.append(sem_dv.bump())
            wtb_done = bc_ms[-1]

            act_dv_ms = []
            gj = 0
            for g in range(NG):
                for j in range(IC):
                    buf = gj % 2
                    gus = sem_gu[buf]
                    slab_gate = act_dv_ms[gj - 2] if gj >= 2 else None

                    def fgu(e, g=g, j=j, buf=buf, gus=gus, slab_gate=slab_gate,
                            rg=regs_done):
                        if slab_gate is not None:
                            e.wait_ge(sem_dv.h, slab_gate)
                        if g == 0 and j == 0:
                            e.wait_ge(sem_rg.h, rg)
                        e.dma_start(
                            out=s_gus[buf][:, 0:KC * 128].rearrange(
                                "p (k m) -> p k m", m=128),
                            in_=sg_gu[g, j]
                        ).then_inc(gus.h, 16)
                        e.dma_start(
                            out=s_gus[buf][:, KC * 128:2 * KC * 128].rearrange(
                                "p (k m) -> p k m", m=128),
                            in_=sg_gu[g, j + IC]
                        ).then_inc(gus.h, 16)
                    p.on("gpsimd", fgu)
                    gu_ready = gus.bump(32)

                    def fup(e, buf=buf, gus=gus, need=gu_ready,
                            bank_gate=act_dv_ms[gj - 1] if gj >= 1 else None):
                        e.wait_ge(gus.h, need)
                        if bank_gate is not None:
                            e.wait_ge(sem_dv.h, bank_gate)
                        ins = None
                        for k in range(KC):
                            e.matmul(ps32[3][:, :],
                                     s_gus[buf][:, k * 128:(k + 1) * 128],
                                     s_x2b[:, k * 512:(k + 1) * 512],
                                     start=(k == 0), stop=(k == KC - 1))
                        for k in range(KC):
                            ins = e.matmul(ps32[4][:, :],
                                           s_gus[buf][:, KC * 128 + k * 128: KC * 128 + (k + 1) * 128],
                                           s_x2b[:, k * 512:(k + 1) * 512],
                                           start=(k == 0), stop=(k == KC - 1))
                        ins.then_inc(sem_pe.h, 1)
                    p.on("tensor", fup)
                    up_pe = sem_pe.bump()

                    def fsilu(e, need=up_pe):
                        e.wait_ge(sem_pe.h, need)
                        e.activation(s_sg[:, :], ps32[3][:, :], AF.Silu
                                     ).then_inc(sem_ac.h, 1)
                    p.on("scalar", fsilu)
                    silu_ms = sem_ac.bump()

                    def fmu(e, need=silu_ms,
                            war=act_dv_ms[gj - 1] if gj >= 1 else None):
                        e.wait_ge(sem_ac.h, need)
                        if war is not None:
                            e.wait_ge(sem_dv.h, war)
                        e.tensor_mul(s_t[:, :], s_sg[:, :], ps32[4][:, :]
                                     ).then_inc(sem_dv.h, 1)
                    p.on("vector", fmu)
                    mu_ms = sem_dv.bump()

                    def fsc(e, g=g, j=j, need=mu_ms, wtbn=wtb_done):
                        e.wait_ge(sem_dv.h, max(need, wtbn))
                        e.tensor_mul(s_act[g][:, j * 512:(j + 1) * 512], s_t[:, :],
                                     s_wtb[:, g * 512:(g + 1) * 512]
                                     ).then_inc(sem_dv.h, 1)
                    p.on("vector", fsc)
                    act_dv_ms.append(sem_dv.bump())
                    gj += 1
            act_all = act_dv_ms[-1]

            dn_ev_ms = []
            last_st = [None, None]
            last_c1 = None
            for oc in range(KC):
                par = oc % 2
                dns = sem_dn[par]
                slab_gate = None if oc < 2 else dn_ev_ms[oc - 2]

                def fdn(e, oc=oc, par=par, dns=dns, slab_gate=slab_gate,
                        rg=regs_done):
                    if slab_gate is not None:
                        e.wait_ge(sem_dv.h, slab_gate)
                    for g in range(NG):
                        e.dma_start(
                            out=s_dns[par][:, g * IC * 128:(g + 1) * IC * 128
                                           ].rearrange("p (k m) -> p k m", m=128),
                            in_=sg_dn[g, oc]
                        ).then_inc(dns.h, 16)
                p.on("gpsimd", fdn)
                dn_ready = dns.bump(16 * NG)

                sxs = sem_sx[par]

                def fsx(e, oc=oc, ph=ph, par=par, sxs=sxs,
                        gate=None if oc < 2 else dn_ev_ms[oc - 2]):
                    if gate is not None:
                        e.wait_ge(sem_dv.h, gate)
                    sl = slice(oc * T + ph * 512, oc * T + (ph + 1) * 512)
                    e.dma_start(out=s_stl[par][:], in_=s_in[:, sl]).then_inc(sxs.h, 16)
                    e.dma_start(out=s_xnt[par][:], in_=xnew_in[:, sl]).then_inc(sxs.h, 16)
                p.on("sync", fsx)
                sx_ready = sxs.bump(32)

                bank = ps32[5 if par == 0 else 2]

                def fdm(e, oc=oc, par=par, dns=dns, need=dn_ready, actn=act_all,
                        bank=bank, bank_gate=None if oc < 2 else dn_ev_ms[oc - 2],
                        wtbn=wtb_done):
                    e.wait_ge(dns.h, need)
                    if oc == 0:
                        e.wait_ge(sem_dv.h, actn)
                    elif oc == 1:
                        e.wait_ge(sem_dv.h, wtbn)
                    if bank_gate is not None:
                        e.wait_ge(sem_dv.h, bank_gate)
                    ins = None
                    for g in range(NG):
                        for kk in range(IC):
                            ins = e.matmul(
                                bank[:, :],
                                s_dns[par][:, (g * IC + kk) * 128:(g * IC + kk + 1) * 128],
                                s_act[g][:, kk * 512:(kk + 1) * 512],
                                start=(g == 0 and kk == 0),
                                stop=(g == NG - 1 and kk == IC - 1))
                    ins.then_inc(sem_pe.h, 1)
                p.on("tensor", fdm)
                dm_pe = sem_pe.bump()

                def fg2(e, par=par, need=dm_pe, bank=bank, war=last_c1):
                    e.wait_ge(sem_pe.h, need)
                    if war is not None:
                        e.wait_ge(sem_dv.h, war)
                    e.activation(s_g2[:, :], bank[:, :], AF.Silu).then_inc(sem_ac.h, 1)
                p.on("scalar", fg2)
                g2_ms = sem_ac.bump()

                def fc1(e, par=par, need=g2_ms, sxn=sx_ready, sxs=sxs,
                        war=dn_ev_ms[oc - 1] if oc >= 1 else act_all):
                    e.wait_ge(sem_ac.h, need)
                    e.wait_ge(sxs.h, sxn)
                    e.wait_ge(sem_dv.h, war)
                    e.tensor_mul(s_t[:, :], s_g2[:, :], s_stl[par][:]
                                 ).then_inc(sem_dv.h, 1)
                p.on("vector", fc1)
                c1_ms = sem_dv.bump()
                last_c1 = c1_ms

                def fc2(e, par=par, need=c1_ms, stw=last_st[par]):
                    e.wait_ge(sem_dv.h, need)
                    if stw is not None:
                        e.wait_ge(sem_st[par].h, stw)
                    e.tensor_add(s_ot[par][:], s_t[:, :], s_xnt[par][:]
                                 ).then_inc(sem_dv.h, 1)
                p.on("vector", fc2)
                c2_ms = sem_dv.bump()
                dn_ev_ms.append(c2_ms)

                def fst(e, oc=oc, ph=ph, par=par, need=c2_ms):
                    e.wait_ge(sem_dv.h, need)
                    sl = slice(oc * T + ph * 512, oc * T + (ph + 1) * 512)
                    e.dma_start(out=out_t[:, sl], in_=s_ot[par][:]
                                ).then_inc(sem_st[par].h, 16)
                p.on("sync", fst)
                last_st[par] = sem_st[par].bump(16)

        p.wait("sync", sem_st[0], sem_st[0].n)
        p.wait("sync", sem_st[1], sem_st[1].n)
        p.play(block)
    return nc



def tile_oc_k(wT):
    K, M = wT.shape
    return np.ascontiguousarray(
        wT.reshape(K // 128, 128, M // 128, 128).transpose(2, 0, 1, 3))


def tile_p_first(wT):
    K, M = wT.shape
    return np.ascontiguousarray(
        wT.reshape(K // 128, 128, M // 128, 128).transpose(2, 1, 0, 3))


def pm(v):
    return np.ascontiguousarray(np.asarray(v).reshape(-1, 128).T)


def cm(x):
    return np.ascontiguousarray(np.asarray(x).T)



_D, _T, _HD = 2048, 1024, 32
_KC = _D // 128


def _pack_cm(x_td):
    return np.ascontiguousarray(
        x_td.T.reshape(_KC, 128, _T).transpose(1, 0, 2).reshape(128, _KC * _T))


def _unpack_cm(a):
    return np.ascontiguousarray(
        np.asarray(a).reshape(128, _KC, _T).transpose(1, 0, 2).reshape(_D, _T).T)


def kernel(L0_attn_x, L0_attn_w_qkv, L0_attn_w_proj, L0_attn_b_proj,
           L0_ffn_weight, L0_ffn_scale_weight, L0_ffn_bias,
           L1_attn_ln_scale, L1_attn_ln_bias, L1_attn_mix_k, L1_attn_mix_v,
           L1_attn_mix_r, L1_attn_time_first, L1_attn_time_decay,
           L1_attn_Wk, L1_attn_Wv, L1_attn_Wr, L1_attn_Wo,
           L1_ffn_gate_weight, L1_ffn_gate_up_proj, L1_ffn_down_proj,
           L1_ffn_merge_groups, L1_ffn_dominant_experts):
    import ml_dtypes
    x = np.asarray(L0_attn_x, np.float32)
    B, S, D = x.shape
    assert (B, S, D) == (4, 2048, 2048)

    f32 = lambda a: np.ascontiguousarray(np.asarray(a, np.float32))
    wqT = f32(L0_attn_w_qkv).T
    wq_t = tile_oc_k(np.ascontiguousarray(wqT[:, 0:D]))
    wkv_k = np.ascontiguousarray(wqT[:, D:2 * D])
    wkv_v = np.ascontiguousarray(wqT[:, 2 * D:3 * D])
    fw_t = tile_oc_k(np.ascontiguousarray(f32(L0_ffn_weight).T))
    wp_t = tile_oc_k(np.ascontiguousarray(f32(L0_attn_w_proj).T))
    wk_t = tile_oc_k(f32(L1_attn_Wk))
    wv_t = tile_oc_k(f32(L1_attn_Wv))
    wr_t = tile_oc_k(f32(L1_attn_Wr))
    wo_t = tile_oc_k(f32(L1_attn_Wo))
    gw_t = np.ascontiguousarray(f32(L1_ffn_gate_weight).T.reshape(_KC, 128, 32))
    gu = f32(L1_ffn_gate_up_proj)
    dn = f32(L1_ffn_down_proj)
    gu_t = np.stack([tile_p_first(np.ascontiguousarray(gu[e].T)) for e in range(32)])
    dn_t = np.stack([tile_p_first(np.ascontiguousarray(dn[e].T)) for e in range(32)])

    ones_col = np.ones((128, 1), np.float32)
    ones_row = np.ones((1, 128), np.float32)
    cst = np.array([[1e-5, 1e-30, 0, 0]], np.float32)
    cst_col = np.repeat(np.array([[1e-15, 1e-30]], np.float32), 128, axis=0)
    ident = np.eye(128, dtype=ml_dtypes.bfloat16)
    iota_row = np.arange(_T, dtype=np.float32)[None, :]
    io8 = np.arange(8, dtype=np.float32)[None, :]
    mg_i = np.ascontiguousarray(np.asarray(L1_ffn_merge_groups).astype(np.int32)[:, None])
    de_i = np.ascontiguousarray(np.asarray(L1_ffn_dominant_experts).astype(np.int32)[None, :])
    zcol = np.zeros((128, _KC), np.float32)

    fscale_pm, fbias_pm = pm(L0_ffn_scale_weight), pm(L0_ffn_bias)
    bproj_pm = pm(L0_attn_b_proj)
    gam_pm, bet_pm = pm(L1_attn_ln_scale), pm(L1_attn_ln_bias)
    mk_pm, mv_pm, mr_pm = pm(L1_attn_mix_k), pm(L1_attn_mix_v), pm(L1_attn_mix_r)
    tf_pm, td_pm = pm(L1_attn_time_first), pm(L1_attn_time_decay)
    td_row = np.ascontiguousarray(np.asarray(L1_attn_time_decay, np.float32)[None, :])

    x_cm = [_pack_cm(x[c // 2, (c % 2) * _T:(c % 2 + 1) * _T, :]) for c in range(NCORES)]

    cores = list(range(NCORES))

    ncA = build_launch_a(D=_D, T=_T, HD=_HD)
    imA = [dict(x_cm=np.ascontiguousarray(
                    x[c // 2, (c % 2) * _T:(c % 2 + 1) * _T, :].T),
                wq_t=wq_t, wkv_k=wkv_k, wkv_v=wkv_v, fw_t=fw_t,
                fscale_pm=fscale_pm, fbias_pm=fbias_pm,
                ones_col=ones_col, ones_row=ones_row, cst=cst)
           for c in cores]
    resA = run_bass_kernel_spmd(ncA, imA, core_ids=cores)
    qs = [r["q_out"] for r in resA.results]
    vks = [np.asarray(r["vk_out"]) for r in resA.results]
    gs = [r["g_out"] for r in resA.results]

    ncB = build_launch_b(D=_D, T=_T, HD=_HD)
    imB = [dict(x_cm=np.ascontiguousarray(
                    x[c // 2, (c % 2) * _T:(c % 2 + 1) * _T, :].T),
                q_in=np.asarray(qs[c]), vk_s=vks[c], vk_p=vks[c ^ 1],
                g_in=np.asarray(gs[c]), wp_t=wp_t, bproj_pm=bproj_pm,
                gam_pm=gam_pm, bet_pm=bet_pm, ident=ident,
                ones_col=ones_col, ones_row=ones_row, cst=cst, cst_col=cst_col)
           for c in cores]
    resB = run_bass_kernel_spmd(ncB, imB, core_ids=cores)
    xnews = [np.asarray(r["xnew_out"]) for r in resB.results]
    xn2s = [np.asarray(r["xn2_out"]) for r in resB.results]
    xlns = [np.asarray(r["xln_out"]) for r in resB.results]

    ncC = build_launch_c(D=_D, T=_T, HD=_HD)
    imC = []
    for c in cores:
        bcol = (np.ascontiguousarray(xlns[c - 1][:, _T - 1::_T]) if c % 2 == 1
                else zcol)
        imC.append(dict(xln_in=xlns[c], bcol=bcol, mk_pm=mk_pm, mv_pm=mv_pm,
                        mr_pm=mr_pm, td_pm=td_pm, wk_t=wk_t, wv_t=wv_t, wr_t=wr_t))
    resC = run_bass_kernel_spmd(ncC, imC, core_ids=cores)

    ncD1 = build_launch_d1(D=_D, T=_T, HD=_HD)
    imD1 = []
    for c in cores:
        rc = resC.results[c]
        acar = (np.asarray(resC.results[c - 1]["al_out"]) if c % 2 == 1 else zcol)
        bcar = (np.asarray(resC.results[c - 1]["bl_out"]) if c % 2 == 1 else zcol)
        imD1.append(dict(a_in=np.asarray(rc["a_out"]), b_in=np.asarray(rc["b_out"]),
                         ek_in=np.asarray(rc["ek_out"]), ekv_in=np.asarray(rc["ekv_out"]),
                         r_in=np.asarray(rc["r_out"]), xn2_in=xn2s[c],
                         acar=np.ascontiguousarray(acar), bcar=np.ascontiguousarray(bcar),
                         tf_pm=tf_pm, td_row=td_row, iota_row=iota_row,
                         wo_t=wo_t, cst_col=cst_col))
    resD1 = run_bass_kernel_spmd(ncD1, imD1, core_ids=cores)

    ncD2 = build_launch_d2(D=_D, T=_T, HD=_HD)
    imD2 = [dict(xnew_in=xnews[c], s_in=np.asarray(resD1.results[c]["s_out"]),
                 xn2_in=xn2s[c], gw_t=gw_t, gu_t=gu_t, dn_t=dn_t,
                 mg_i=mg_i, de_i=de_i, io8=io8, ident=ident, ones_row=ones_row)
            for c in cores]
    resD2 = run_bass_kernel_spmd(ncD2, imD2, core_ids=cores)

    out = np.empty_like(x)
    for c in cores:
        out[c // 2, (c % 2) * _T:(c % 2 + 1) * _T, :] = _unpack_cm(
            resD2.results[c]["out"])
    return out
